# version 03
# baseline (speedup 1.0000x reference)
import numpy as np
import ml_dtypes
from contextlib import ExitStack

import concourse.bass as bass
import concourse.mybir as mybir
from concourse.bass_utils import run_bass_kernel_spmd

F32 = mybir.dt.float32
BF16 = mybir.dt.bfloat16
I32 = mybir.dt.int32
U32 = mybir.dt.uint32
AF = mybir.ActivationFunctionType
ALU = mybir.AluOpType
AX = mybir.AxisListType
BF = ml_dtypes.bfloat16

B, T, D = 4, 2048, 2048
NCORES = 8
MAGIC = 12582912.0

ENGINES = ("sync", "tensor", "vector", "scalar", "gpsimd")


class Sem:

    def __init__(self, es, nc, name):
        self.h = es.enter_context(nc.semaphore(name))
        self.n = 0


class Pending:
    __slots__ = ("meth", "args", "kwargs", "incs", "gidx", "gidx_serial",
                 "dma_target", "eng", "prog")

    def __init__(self, meth, args, kwargs, eng=None, prog=None):
        self.meth, self.args, self.kwargs = meth, args, kwargs
        self.incs = []
        self.eng = eng
        self.prog = prog

    def raw_inc(self, sem, k=1):
        self.incs.append((sem, k))
        sem.n += k
        return self

    def inc(self, sem, k=1):
        if (self.eng in ("vector", "scalar", "gpsimd")
                and self.meth != "dma_start" and self.prog is not None):
            self.prog.q[self.eng]
            rec = getattr(self.prog, self.eng)
            rec.finc(sem, k)
            return self
        return self.raw_inc(sem, k)

    def run(self, eng):
        inst = getattr(eng, self.meth)(*self.args, **self.kwargs)
        for sem, k in self.incs:
            inst.then_inc(sem.h, k)
        return inst


class Wait:
    __slots__ = ("sem", "val")

    def __init__(self, sem, val):
        self.sem, self.val = sem, val

    def run(self, eng):
        eng.wait_ge(self.sem.h, self.val)


class EngRec:

    def __init__(self, prog, name):
        self._prog, self._name = prog, name

    def wait(self, sem, val=None):
        self._prog.q[self._name].append(Wait(sem, sem.n if val is None else val))

    def barrier(self, sem, pad=None):
        if pad is None:
            pad = PTR_FENCE_PAD
        if pad:
            fb = self._prog.ctx.fencebuf()
            w = min(pad, fb.shape[1])
            p = Pending("memset", (fb[:, 0:w], 0.0), {})
            if self._name == "scalar":
                p = Pending("memzero", (fb[:, 0:w],), {})
            elif self._name in ("tensor", "sync"):
                p = Pending("nop", (), {})
            p.gidx = self._prog.gidx
            self._prog.gidx += 1
            self._prog.q[self._name].append(p)
        self.finc(sem)
        self.wait(sem)

    def finc(self, sem, k=1):
        fb = self._prog.ctx.fencebuf()
        if self._name in ("tensor", "sync"):
            p = Pending("nop", (), {})
        elif self._name == "scalar":
            p = Pending("memzero", (fb[:, 1:2],), {})
        elif self._name == "vector":
            p = Pending("memset", (fb[:, 0:1], 0.0), {})
        else:
            p = Pending("memset", (fb[:, 2:3], 0.0), {})
        p.gidx = self._prog.gidx
        self._prog.gidx += 1
        self._prog.q[self._name].append(p)
        return p.raw_inc(sem, k)

    def __getattr__(self, meth):
        prog, name = self._prog, self._name

        def rec(*args, **kwargs):
            p = Pending(meth, args, kwargs, eng=name, prog=prog)
            p.gidx = prog.gidx
            prog.gidx += 1
            prog.q[name].append(p)
            return p

        return rec


SERIAL_DEBUG = False
SERIAL_CUTOFF = 10 ** 9
PTR_FENCE_PAD = 0
PHASE_MARKS = {}
DEBUG_L0A2 = False
DEBUG_SCRATCH = False


class Prog:
    def __init__(self, nc, es):
        self.nc, self.es = nc, es
        self.q = {k: [] for k in ENGINES}
        self.gidx = 0
        self.sync = EngRec(self, "sync")
        self.tensor = EngRec(self, "tensor")
        self.vector = EngRec(self, "vector")
        self.scalar = EngRec(self, "scalar")
        self.gpsimd = EngRec(self, "gpsimd")
        self._semid = 0

    def sem(self, name=None):
        self._semid += 1
        return Sem(self.es, self.nc, name or f"s{self._semid}")

    def mark(self, name):
        PHASE_MARKS[name] = self.gidx

    def flush(self):
        if SERIAL_DEBUG:
            self._flush_serial()
            return
        with self.nc.Block() as blk:
            for name in ENGINES:
                ops = self.q[name]
                if not ops:
                    continue

                def body(eng, ops=ops):
                    for op in ops:
                        op.run(eng)

                getattr(blk, name)(body)
        self.q = {k: [] for k in ENGINES}

    def _flush_serial(self):
        gsem = self.sem("gserial")
        fb = self.ctx.fencebuf()
        done = {}
        step = [0]

        def mkbody(name, ops):
            def body(eng, ops=ops, name=name):
                for op in ops:
                    if isinstance(op, Wait):
                        op.run(eng)
                        continue
                    if op.gidx_serial >= SERIAL_CUTOFF:
                        op.run(eng)
                        continue
                    eng.wait_ge(gsem.h, min(op.gidx_serial, SERIAL_CUTOFF))
                    inst = op.run(eng)
                    if op.meth == "dma_start":
                        for sem, k in op.incs:
                            eng.wait_ge(sem.h, op.dma_target)
                        eng.nop().then_inc(gsem.h, 1)
                    elif name == "tensor":
                        eng.nop().then_inc(gsem.h, 1)
                    elif name == "sync":
                        eng.nop().then_inc(gsem.h, 1)
                    elif name == "scalar":
                        eng.copy(fb[0:1, 2:3], fb[0:1, 3:4]
                                 ).then_inc(gsem.h, 1)
                    else:
                        eng.tensor_copy(fb[0:1, 2:3], fb[0:1, 3:4]
                                        ).then_inc(gsem.h, 1)
            return body

        allops = []
        for name in ENGINES:
            for op in self.q[name]:
                if isinstance(op, Pending):
                    allops.append(op)
        allops.sort(key=lambda o: o.gidx)
        dmacount = {}
        for i, op in enumerate(allops):
            op.gidx_serial = i
            if op.meth == "dma_start":
                for sem, k in op.incs:
                    dmacount[id(sem)] = dmacount.get(id(sem), 0) + k
                    op.dma_target = dmacount[id(sem)]
        with self.nc.Block() as blk:
            for name in ENGINES:
                if self.q[name]:
                    getattr(blk, name)(mkbody(name, self.q[name]))
        self.q = {k: [] for k in ENGINES}


def f32_to_bf16_np(a):
    return np.asarray(a, np.float32).astype(BF)


class Ctx:

    def __init__(self, nc, es, p):
        self.nc, self.es, self.p = nc, es, p
        self.sb_tensors = {}
        self._fb = None
        p.ctx = self

    def fencebuf(self):
        if self._fb is None:
            self._fb = self.sb("fencebuf", [128, 8], F32)
        return self._fb

    def sb(self, name, shape, dt):
        t = self.es.enter_context(self.nc.sbuf_tensor(name, shape, dt))
        self.sb_tensors[name] = t
        return t

    def ps(self, name, shape, dt=F32):
        return self.es.enter_context(self.nc.psum_tensor(name, shape, dt))

    def dram(self, name, shape, dt, kind="Internal"):
        return self.nc.dram_tensor(name, shape, dt, kind=kind)


class PsumPool:

    def __init__(self, ctx, name, n, shape=(128, 512), dt=F32):
        self.slots = [ctx.ps(f"{name}{i}", list(shape), dt) for i in range(n)]
        self.guards = [None] * n
        self.i = 0
        self.n = n

    def acquire(self, p_tensor):
        i = self.i
        self.i = (self.i + 1) % self.n
        g = self.guards[i]
        if g is not None:
            p_tensor.wait(g[0], g[1])
            self.guards[i] = None
        return self.slots[i], i

    def release(self, i, sem, count=None):
        self.guards[i] = (sem, sem.n if count is None else count)


def make_identity_bf16():
    return np.eye(128).astype(BF)


def make_blockones(nh=16, hd=128):
    m = np.zeros((nh, nh * hd), np.float32)
    for h in range(nh):
        m[h, h * hd:(h + 1) * hd] = 1.0
    return m


def as_np(a, dt=np.float32):
    return np.ascontiguousarray(np.asarray(a, dt))


_BUILD_CACHE = {}


def cached_build(key, fn):
    if key not in _BUILD_CACHE:
        _BUILD_CACHE[key] = fn()
    return _BUILD_CACHE[key]




class DmaBuf:

    def __init__(self, p, name, tensor):
        self.t = tensor
        self.sem = p.sem(name + "_dma")
        self.guard = None

    def load(self, queue, view, out=None, transpose=False):
        if self.guard is not None:
            queue.wait(self.guard[0], self.guard[1])
            self.guard = None
        dst = self.t[:, :] if out is None else out
        if transpose:
            queue.dma_start(out=dst, in_=view, transpose=True).inc(self.sem, 16)
        else:
            queue.dma_start(out=dst, in_=view).inc(self.sem, 16)

    def store(self, queue, dram_view, src=None):
        if self.guard is not None:
            queue.wait(self.guard[0], self.guard[1])
            self.guard = None
        s = self.t[:, :] if src is None else src
        queue.dma_start(out=dram_view, in_=s).inc(self.sem, 16)

    def wait(self, eng):
        eng.wait(self.sem)

    def consumed(self, sem, count=None):
        self.guard = (sem, sem.n if count is None else count)

    def stored(self):
        self.guard = (self.sem, self.sem.n)


def emit_ln_prelude(ctx, x_dram, ntile, x_fm, xln_scr, sems, row0=0,
                    eps=1e-5, prewait=(), stat_dump=None):
    p = ctx.p
    s_v, s_s = sems["v"], sems["s"]
    D_ = 2048
    xt = [DmaBuf(p, f"lnx{k}", ctx.sb(f"lnx{k}", [128, D_], F32))
          for k in range(2)]
    xc = [ctx.sb(f"lnc{k}", [128, D_], F32) for k in range(2)]
    xb = [DmaBuf(p, f"lnb{k}", ctx.sb(f"lnb{k}", [128, D_], BF16))
          for k in range(2)]
    stat = ctx.sb("lnstat", [128, 8], F32)

    for i in range(ntile):
        k = i % 2
        xt[k].load(p.sync, x_dram[i * 128:(i + 1) * 128, :])
        xt[k].wait(p.vector)
        p.vector.reduce_sum(stat[:, 0:1], xt[k].t[:, :], axis=AX.X)
        p.vector.barrier(s_v)
        p.vector.tensor_scalar(stat[:, 1:2], stat[:, 0:1], 1.0 / D_, None,
                               ALU.mult)
        p.vector.barrier(s_v)
        p.vector.tensor_scalar(xc[k][:, :], xt[k].t[:, :], stat[:, 1:2], None,
                               ALU.subtract)
        p.vector.finc(s_v)
        p.scalar.wait(s_v)
        p.scalar.activation(xt[k].t[:, :], xc[k][:, :], AF.Square,
                            accum_out=stat[:, 2:3])
        p.scalar.finc(s_s)
        xt[k].consumed(s_s)
        p.vector.wait(s_s)
        p.vector.tensor_scalar(stat[:, 3:4], stat[:, 2:3], 1.0 / D_, eps,
                               ALU.mult, ALU.add)
        p.vector.barrier(s_v)
        p.vector.reciprocal(stat[:, 4:5], stat[:, 3:4])
        p.vector.finc(s_v)
        p.scalar.wait(s_v)
        p.scalar.activation(stat[:, 5:6], stat[:, 4:5], AF.Sqrt)
        p.scalar.finc(s_s)
        p.vector.wait(s_s)
        p.vector.barrier(s_v)
        if xb[k].guard is not None:
            p.vector.wait(*xb[k].guard)
            xb[k].guard = None
        p.vector.tensor_scalar(xb[k].t[:, :], xc[k][:, :], stat[:, 5:6], None,
                               ALU.mult)
        p.vector.finc(s_v)
        p.sync.wait(s_v)
        xb[k].store(p.sync, xln_scr[row0 + i * 128:row0 + (i + 1) * 128, :])
        xb[k].stored()
        if stat_dump is not None:
            sdb = DmaBuf(p, f"sdump{i}", None)
            sdb.store(p.sync, stat_dump[i], src=stat[:, 0:8])
    for bq in xb:
        bq.wait(p.sync)
    for bq in prewait:
        bq.wait(p.sync)
    nldr = row0 + ntile * 128
    ldb = DmaBuf(p, "xfmld", None)
    for c in range(16):
        ldb.load(p.sync, xln_scr[0:nldr, c * 128:(c + 1) * 128],
                 out=x_fm[:, c, 0:nldr], transpose=True)
    return ldb


class WBufPool:

    def __init__(self, ctx, name, kc, nmax=512, n=2, dt=BF16):
        self.bufs = [DmaBuf(ctx.p, f"{name}{i}",
                            ctx.sb(f"{name}{i}", [128, kc, nmax], dt))
                    for i in range(n)]
        self.i = 0
        self.n = n

    def load(self, p, wT_view, consumer_sem):
        b = self.bufs[self.i]
        self.i = (self.i + 1) % self.n
        kc, nw = wT_view.shape[1], wT_view.shape[2]
        b.load(p.gpsimd, wT_view, out=b.t[:, 0:kc, 0:nw])
        return b

    def wait(self, b, p_tensor):
        b.wait(p_tensor)


def wT_chunk_view(wT_dram, k_dim, c0, nw, kb0=0, kbn=None):
    kc = k_dim // 128
    v = wT_dram.rearrange("(c p) n -> p c n", p=128)
    if kbn is None:
        kbn = kc
    return v[:, kb0:kb0 + kbn, c0:c0 + nw]


def emit_proj(ctx, x_fm, pad, ntile, wT_dram, k_dim, n_out, sems, pools,
              epilogue, bias_row=None, ones1=None, nchunk=512):
    p = ctx.p
    s_pe = sems["pe"]
    wpool, ppool = pools["w"], pools["ps"]
    kc = k_dim // 128
    p.tensor.wait(sems["v"])
    p.tensor.wait(sems["s"])
    p.tensor.wait(sems["g"])
    for c0 in range(0, n_out, nchunk):
        nw = min(nchunk, n_out - c0)
        wb = wpool.load(p, wT_chunk_view(wT_dram, k_dim, c0, nw), s_pe)
        wb.wait(p.tensor)
        for i in range(ntile):
            ps, slot = ppool.acquire(p.tensor)
            for c in range(kc):
                mm = p.tensor.matmul(ps[:, 0:nw],
                                     x_fm[:, c, pad + i * 128: pad + (i + 1) * 128],
                                     wb.t[:, c, 0:nw], start=(c == 0),
                                     stop=(c == kc - 1 and bias_row is None))
            if bias_row is not None:
                mm = p.tensor.matmul(ps[:, 0:nw], ones1[:, 0:128],
                                     bias_row[:, c0:c0 + nw], start=False,
                                     stop=True)
            mm.inc(s_pe)
            epilogue(i, c0, nw, ps, slot, ppool)
        wb.consumed(s_pe)


NT8 = 8
NTOK = 1024


def _common_ctx():
    nc = bass.Bass("TRN2", target_bir_lowering=False, debug=False,
                   detect_race_conditions=False)
    es = ExitStack()
    p = Prog(nc, es)
    ctx = Ctx(nc, es, p)
    return nc, es, p, ctx


def _basic_sems(p):
    return {
        "const": p.sem("const"),
        "pe": p.sem("pe"),
        "v": p.sem("v"),
        "s": p.sem("s"),
        "g": p.sem("g"),
    }


def emit_halo_ln(ctx, halo, xln_scr, sems):
    p = ctx.p
    s_v, s_s = sems["v"], sems["s"]
    hx = ctx.sb("hx", [3, D], F32)
    hxb = DmaBuf(p, "hxb", ctx.sb("hxb", [3, D], BF16))
    hst = ctx.sb("hst", [3, 8], F32)
    hld = DmaBuf(p, "hld", hx)
    hld.load(p.sync, halo)
    hld.wait(p.vector)
    p.vector.reduce_sum(hst[:, 0:1], hx[:, :], axis=AX.X)
    p.vector.barrier(s_v)
    p.vector.tensor_scalar(hst[:, 1:2], hst[:, 0:1], 1.0 / D, None, ALU.mult)
    p.vector.barrier(s_v)
    p.vector.tensor_scalar(hx[:, :], hx[:, :], hst[:, 1:2], None, ALU.subtract)
    p.vector.finc(s_v)
    p.scalar.wait(s_v)
    p.scalar.activation(hxb.t[:, :], hx[:, :], AF.Square,
                        accum_out=hst[:, 2:3])
    p.scalar.finc(s_s)
    p.vector.wait(s_s)
    p.vector.tensor_scalar(hst[:, 3:4], hst[:, 2:3], 1.0 / D, 1e-5,
                           ALU.mult, ALU.add)
    p.vector.barrier(s_v)
    p.vector.reciprocal(hst[:, 4:5], hst[:, 3:4])
    p.vector.finc(s_v)
    p.scalar.wait(s_v)
    p.scalar.activation(hst[:, 5:6], hst[:, 4:5], AF.Sqrt)
    p.scalar.finc(s_s)
    p.vector.wait(s_s)
    p.vector.barrier(s_v)
    p.vector.tensor_scalar(hxb.t[:, :], hx[:, :], hst[:, 5:6], None, ALU.mult)
    p.vector.finc(s_v)
    p.sync.wait(s_v)
    hxb.store(p.sync, xln_scr[125:128, :])
    hxb.stored()
    return hxb


def build_l0a1():
    nc, es, p, ctx = _common_ctx()
    ntile, ntok = NT8, NTOK
    x = ctx.dram("x", [ntok, D], F32, kind="ExternalInput").ap()
    halo = ctx.dram("halo", [3, D], F32, kind="ExternalInput").ap()
    wqT = ctx.dram("wqT", [D, D], F32, kind="ExternalInput").ap()
    wkT = ctx.dram("wkT", [D, D], F32, kind="ExternalInput").ap()
    wvT = ctx.dram("wvT", [D, D], F32, kind="ExternalInput").ap()
    wigT = ctx.dram("wigT", [D, D], F32, kind="ExternalInput").ap()
    wogT = ctx.dram("wogT", [D, D], F32, kind="ExternalInput").ap()
    wgT = ctx.dram("wgT", [D, 16], F32, kind="ExternalInput").ap()
    b_ig = ctx.dram("b_ig", [1, D], F32, kind="ExternalInput").ap()
    b_og = ctx.dram("b_og", [1, D], F32, kind="ExternalInput").ap()
    b_g = ctx.dram("b_g", [1, 16], F32, kind="ExternalInput").ap()
    conv_w = ctx.dram("conv_w", [D, 4], F32, kind="ExternalInput").ap()
    conv_bT = ctx.dram("conv_bT", [128, 16], F32, kind="ExternalInput").ap()
    onehotd = ctx.dram("onehot", [16, D], F32, kind="ExternalInput").ap()

    qn_scr = ctx.dram("qn_scr", [ntok, D], BF16, kind="ExternalOutput").ap()
    og_scr = ctx.dram("og_scr", [ntok, D], BF16, kind="ExternalOutput").ap()
    bt_scr = ctx.dram("bt_scr", [ntok, D], BF16, kind="ExternalOutput").ap()
    gamT_out = ctx.dram("gamT_out", [16, ntok], F32, kind="ExternalOutput").ap()
    last_state = ctx.dram("last_state", [128, 16], F32, kind="ExternalOutput").ap()
    xln_scr = ctx.dram("xln_scr", [128 + ntok, D], BF16, kind="Internal").ap()

    sems = _basic_sems(p)
    s_const, s_pe = sems["const"], sems["pe"]
    s_v, s_s, s_g = sems["v"], sems["s"], sems["g"]

    PAD = 128
    x_fm = ctx.sb("x_fm", [128, 16, PAD + ntok], BF16)
    xconv = ctx.sb("xconv", [128, 16, ntok], BF16)
    onehot = ctx.sb("onehotsb", [16, D], F32)
    ones1 = ctx.sb("ones1", [1, 128], F32)
    onesrow = ctx.sb("onesrow", [1, 512], F32)
    cw = ctx.sb("cw", [128, 16, 4], F32)
    cb = ctx.sb("cb", [128, 16], F32)
    bias_ig = ctx.sb("bias_ig", [1, D], F32)
    bias_og = ctx.sb("bias_og", [1, D], F32)
    bias_g = ctx.sb("bias_g", [1, 16], F32)
    gamT = ctx.sb("gamT", [16, ntok], F32)
    last_sb = ctx.sb("last_sb", [128, 16], F32)

    pool_mm = PsumPool(ctx, "pm", 4)
    pool_g = PsumPool(ctx, "pg", 2, (16, 512), F32)

    for dst, srcd in ((onehot, onehotd), (cb, conv_bT), (bias_ig, b_ig),
                      (bias_og, b_og), (bias_g, b_g)):
        p.sync.dma_start(out=dst[:, :], in_=srcd).inc(s_const, 16)
    p.sync.dma_start(out=cw[:, :, :],
                     in_=conv_w.rearrange("(c p) k -> p c k", p=128)
                     ).inc(s_const, 16)
    p.gpsimd.memset(ones1[:, :], 1.0)
    p.gpsimd.memset(onesrow[:, :], 1.0).inc(s_g, 1)
    const_g = s_g.n

    p.mark("consts")
    hxb = emit_halo_ln(ctx, halo, xln_scr, sems)
    zv = x_fm.ap()[0:125, 0:2, 0:1024]
    p.gpsimd.wait(s_g, const_g)
    p.gpsimd.memset(zv, 0.0).inc(s_g)
    zbuf = DmaBuf(p, "zbuf", None)
    p.sync.wait(s_g)
    zbuf.store(p.sync, xln_scr[0:125, :].rearrange("r (a b) -> r a b", a=2),
               src=zv)
    p.mark("halo")
    xfmld = emit_ln_prelude(ctx, x, ntile, x_fm, xln_scr, sems, row0=PAD,
                            prewait=(zbuf, hxb))
    p.mark("prelude")

    acc = [ctx.sb(f"cacc{k}", [128, ntok], F32) for k in range(2)]
    xfmld.wait(p.vector)
    p.vector.wait(s_g, const_g)
    p.vector.wait(s_const)
    for c in range(16):
        p.vector.tensor_scalar(acc[0][:, :], x_fm[:, c, PAD - 3:PAD - 3 + ntok],
                               cw[:, c, 0:1], None, ALU.mult)
        p.vector.scalar_tensor_tensor(acc[1][:, :], x_fm[:, c, PAD - 2:PAD - 2 + ntok],
                                      cw[:, c, 1:2], acc[0][:, :],
                                      ALU.mult, ALU.add)
        p.vector.scalar_tensor_tensor(acc[0][:, :], x_fm[:, c, PAD - 1:PAD - 1 + ntok],
                                      cw[:, c, 2:3], acc[1][:, :],
                                      ALU.mult, ALU.add)
        p.vector.scalar_tensor_tensor(acc[1][:, :], x_fm[:, c, PAD:PAD + ntok],
                                      cw[:, c, 3:4], acc[0][:, :],
                                      ALU.mult, ALU.add).inc(s_v)
        p.scalar.wait(s_v)
        p.scalar.activation(xconv[:, c, :], acc[1][:, :], AF.Silu,
                            bias=cb[:, c:c + 1])
        p.scalar.finc(s_s)
        p.vector.wait(s_s)

    p.mark("conv")
    CW = 256
    wk_b = DmaBuf(p, "wk_b", ctx.sb("wk_b", [128, 16, CW], BF16))
    wv_b = DmaBuf(p, "wv_b", ctx.sb("wv_b", [128, 16, CW], BF16))
    wig_b = DmaBuf(p, "wig_b", ctx.sb("wig_b", [128, 16, CW], BF16))
    k_s = ctx.sb("k_s", [128, CW], F32)
    v_s = ctx.sb("v_s", [128, CW], F32)
    ig_s = ctx.sb("ig_s", [128, CW], F32)
    bt1 = ctx.sb("bt1", [128, CW], F32)
    ksq = acc[0].ap()[:, 0:CW]
    kst = acc[0].ap()[:, CW:CW + 16]
    btb = [DmaBuf(p, f"btb{k}", ctx.sb(f"btb{k}", [128, CW], BF16))
           for k in range(2)]

    first = True
    for c0 in range(0, D, CW):
        wk_b.load(p.gpsimd, wT_chunk_view(wkT, D, c0, CW))
        wv_b.load(p.gpsimd, wT_chunk_view(wvT, D, c0, CW))
        wig_b.load(p.gpsimd, wT_chunk_view(wigT, D, c0, CW))
        for bufx in (wk_b, wv_b, wig_b):
            bufx.wait(p.tensor)
        if first:
            p.tensor.wait(s_s)
            p.tensor.wait(s_v)
            p.tensor.wait(s_g, const_g)
            p.tensor.wait(s_const)
            xfmld.wait(p.tensor)
            first = False
        for i in range(ntile):
            tsl = x_fm[:, :, PAD + i * 128:PAD + (i + 1) * 128]
            psK, slK = pool_mm.acquire(p.tensor)
            for c in range(16):
                p.tensor.matmul(psK[:, 0:CW], tsl[:, c, :], wk_b.t[:, c, :],
                                start=(c == 0), stop=(c == 15))
            psV, slV = pool_mm.acquire(p.tensor)
            for c in range(16):
                p.tensor.matmul(psV[:, 0:CW], tsl[:, c, :], wv_b.t[:, c, :],
                                start=(c == 0), stop=(c == 15))
            psI, slI = pool_mm.acquire(p.tensor)
            for c in range(16):
                p.tensor.matmul(psI[:, 0:CW], xconv[:, c, i * 128:(i + 1) * 128],
                                wig_b.t[:, c, :], start=(c == 0), stop=False)
            p.tensor.matmul(psI[:, 0:CW], ones1[:, 0:128],
                            bias_ig[:, c0:c0 + CW], start=False, stop=True
                            ).inc(s_pe)
            kslot = i % 2
            p.scalar.wait(s_pe)
            p.scalar.wait(s_v)
            p.scalar.activation(ig_s[:, :], psI[:, 0:CW], AF.Sigmoid).inc(s_s)
            pool_mm.release(slI, s_s)
            p.scalar.copy(k_s[:, :], psK[:, 0:CW])
            for j in range(CW // 128):
                p.scalar.activation(ksq[:, j * 128:(j + 1) * 128],
                                    k_s[:, j * 128:(j + 1) * 128], AF.Square,
                                    accum_out=kst[:, j:j + 1])
            p.scalar.barrier(s_s)
            p.scalar.activation(kst[:, 4:4 + CW // 128], kst[:, 0:CW // 128],
                                AF.Sqrt).inc(s_s)
            pool_mm.release(slK, s_s)
            p.vector.wait(s_pe)
            p.vector.tensor_copy(v_s[:, :], psV[:, 0:CW]).inc(s_v)
            pool_mm.release(slV, s_v)
            p.vector.wait(s_s)
            p.vector.tensor_scalar(kst[:, 4:4 + CW // 128],
                                   kst[:, 4:4 + CW // 128], 1e-12, None,
                                   ALU.max)
            p.vector.barrier(s_v)
            p.vector.reciprocal(kst[:, 6:6 + CW // 128],
                                kst[:, 4:4 + CW // 128])
            p.vector.barrier(s_v)
            for j in range(CW // 128):
                p.vector.tensor_scalar(k_s[:, j * 128:(j + 1) * 128],
                                       k_s[:, j * 128:(j + 1) * 128],
                                       kst[:, 6 + j:7 + j], None, ALU.mult)
            if btb[kslot].guard is not None:
                p.vector.wait(*btb[kslot].guard)
                btb[kslot].guard = None
            p.vector.tensor_mul(bt1[:, :], ig_s[:, :], k_s[:, :])
            p.vector.tensor_mul(btb[kslot].t[:, :], bt1[:, :], v_s[:, :]
                                ).inc(s_v)
            p.sync.wait(s_v)
            btb[kslot].store(p.sync,
                             bt_scr[i * 128:(i + 1) * 128, c0:c0 + CW])
            btb[kslot].stored()
        for bufx in (wk_b, wv_b, wig_b):
            bufx.consumed(s_pe)

    wq_pool = WBufPool(ctx, "wq_b", 16, 512, 1)
    qf = ctx.sb("qf", [128, 512], F32)
    qsq = acc[1].ap()[:, 0:512]
    qs = ctx.sb("qs", [128, 16], F32)
    qnb = [DmaBuf(p, "qnb0", k_s.ap().bitcast(BF16)),
           DmaBuf(p, "qnb1", v_s.ap().bitcast(BF16))]

    def epi_q(i, c0, nw, ps, slot, ppool):
        kslot = i % 2
        p.scalar.wait(s_pe)
        p.scalar.wait(s_v)
        p.scalar.copy(qf[:, 0:nw], ps[:, 0:nw]).inc(s_s)
        ppool.release(slot, s_s)
        for j in range(4):
            p.scalar.activation(qsq[:, j * 128:(j + 1) * 128],
                                qf[:, j * 128:(j + 1) * 128], AF.Square,
                                accum_out=qs[:, j:j + 1])
        p.scalar.finc(s_s)
        p.vector.wait(s_s)
        p.vector.tensor_scalar(qs[:, 4:8], qs[:, 0:4], 0.0, None, ALU.add
                               ).inc(s_v)
        p.scalar.wait(s_v)
        p.scalar.activation(qs[:, 8:12], qs[:, 4:8], AF.Sqrt).inc(s_s)
        p.vector.wait(s_s)
        p.vector.tensor_scalar(qs[:, 8:12], qs[:, 8:12], 1e-12, None, ALU.max)
        p.vector.barrier(s_v)
        p.vector.reciprocal(qs[:, 12:16], qs[:, 8:12])
        p.vector.barrier(s_v)
        if qnb[kslot].guard is not None:
            p.vector.wait(*qnb[kslot].guard)
            qnb[kslot].guard = None
        for j in range(4):
            pv_ = p.vector.tensor_scalar(qnb[kslot].t[:, j * 128:(j + 1) * 128],
                                         qf[:, j * 128:(j + 1) * 128],
                                         qs[:, 12 + j:13 + j], None, ALU.mult)
        pv_.inc(s_v)
        p.sync.wait(s_v)
        qnb[kslot].store(p.sync, qn_scr[i * 128:(i + 1) * 128, c0:c0 + nw])
        qnb[kslot].stored()

    p.mark("c1")
    emit_proj(ctx, x_fm, PAD, ntile, wqT, D, D, sems,
              {"w": wq_pool, "ps": pool_mm}, epi_q)
    p.mark("c2q")

    ogb = [DmaBuf(p, "ogb0", ig_s.ap().bitcast(BF16)),
           DmaBuf(p, "ogb1", bt1.ap().bitcast(BF16))]

    def epi_og(i, c0, nw, ps, slot, ppool):
        kslot = i % 2
        p.scalar.wait(s_pe)
        if ogb[kslot].guard is not None:
            p.scalar.wait(*ogb[kslot].guard)
            ogb[kslot].guard = None
        p.scalar.activation(ogb[kslot].t[:, 0:nw], ps[:, 0:nw], AF.Sigmoid
                            ).inc(s_s)
        ppool.release(slot, s_s)
        p.sync.wait(s_s)
        ogb[kslot].store(p.sync, og_scr[i * 128:(i + 1) * 128, c0:c0 + nw])
        ogb[kslot].stored()

    emit_proj(ctx, xconv, 0, ntile, wogT, D, D, sems,
              {"w": wq_pool, "ps": pool_mm}, epi_og,
              bias_row=bias_og, ones1=ones1)
    p.mark("c3og")

    wg_b = DmaBuf(p, "wg_b", ctx.sb("wg_b", [128, 16, 16], BF16))
    wg_b.load(p.gpsimd, wT_chunk_view(wgT, D, 0, 16))
    wg_b.wait(p.tensor)
    for tc in range(0, ntok, 512):
        psG, gsl = pool_g.acquire(p.tensor)
        for c in range(16):
            p.tensor.matmul(psG[0:16, 0:512], wg_b.t[:, c, 0:16],
                            xconv[:, c, tc:tc + 512], start=(c == 0),
                            stop=False)
        p.tensor.matmul(psG[0:16, 0:512], bias_g[:, 0:16],
                        onesrow[:, 0:512], start=False, stop=True).inc(s_pe)
        p.scalar.wait(s_pe)
        p.scalar.activation(gamT[0:16, tc:tc + 512], psG[0:16, 0:512],
                            AF.Sigmoid)
        p.scalar.finc(s_s)
        pool_g.release(gsl, s_s)

    p.mark("c4g")
    gb = acc[0].ap()
    accb = acc[1].ap().bitcast(BF16)
    btT = DmaBuf(p, "btT", None)
    btT_ap = accb[:, 0:NTOK]
    mem0 = accb[:, NTOK:2 * NTOK]
    for h in range(16):
        p.tensor.wait(s_v)
        for c0 in range(0, NTOK, 512):
            psg, gslot = pool_mm.acquire(p.tensor)
            p.tensor.matmul(psg[:, 0:512], onehot[0:16, h * 128:(h + 1) * 128],
                            gamT[0:16, c0:c0 + 512], start=True, stop=True
                            ).inc(s_pe)
            p.vector.wait(s_pe)
            p.vector.tensor_copy(gb[:, c0:c0 + 512], psg[:, 0:512]).inc(s_v)
            pool_mm.release(gslot, s_v)
        for bq in btb:
            bq.wait(p.sync)
        btT.load(p.sync, bt_scr[:, h * 128:(h + 1) * 128], out=btT_ap,
                 transpose=True)
        btT.wait(p.vector)
        p.vector.tensor_tensor_scan(mem0, gb, btT_ap, 0.0, ALU.mult, ALU.add)
        p.vector.barrier(s_v)
        p.vector.tensor_copy(last_sb[:, h:h + 1], mem0[:, NTOK - 1:NTOK]
                             ).inc(s_v)
        btT.consumed(s_v)

    p.mark("scan")
    p.sync.wait(s_v)
    outs = DmaBuf(p, "outs", None)
    outs.store(p.sync, gamT_out, src=gamT[:, :])
    outs.store(p.sync, last_state, src=last_sb[:, :])
    outs.wait(p.sync)
    p.flush()
    es.close()
    return nc


def build_l0a2():
    nc, es, p, ctx = _common_ctx()
    ntile, ntok = NT8, NTOK
    x = ctx.dram("x", [ntok, D], F32, kind="ExternalInput").ap()
    qn_scr = ctx.dram("qn_scr", [ntok, D], BF16, kind="ExternalInput").ap()
    og_scr = ctx.dram("og_scr", [ntok, D], BF16, kind="ExternalInput").ap()
    bt_scr = ctx.dram("bt_scr", [ntok, D], BF16, kind="ExternalInput").ap()
    gamT_in = ctx.dram("gamT_in", [16, ntok], F32, kind="ExternalInput").ap()
    carry = ctx.dram("carry", [128, 16], F32, kind="ExternalInput").ap()
    woT = ctx.dram("woT", [D, D], F32, kind="ExternalInput").ap()
    mn_w = ctx.dram("mn_w", [1, D], F32, kind="ExternalInput").ap()
    mn_b = ctx.dram("mn_b", [1, D], F32, kind="ExternalInput").ap()
    gn_w = ctx.dram("gn_w", [1, D], F32, kind="ExternalInput").ap()
    gn_b = ctx.dram("gn_b", [1, D], F32, kind="ExternalInput").ap()
    identd = ctx.dram("ident", [128, 128], BF16, kind="ExternalInput").ap()
    identfd = ctx.dram("identf", [128, 128], F32, kind="ExternalInput").ap()
    onehotd = ctx.dram("onehot", [16, D], F32, kind="ExternalInput").ap()

    xnew = ctx.dram("xnew", [ntok, D], F32, kind="ExternalOutput").ap()
    mem_scr = ctx.dram("mem_scr", [D, ntok], BF16, kind="Internal").ap()
    if DEBUG_L0A2:
        dbg_st = ctx.dram("dbg_st", [4, 16, 128], F32, kind="ExternalOutput").ap()
        dbg_t2 = ctx.dram("dbg_t2", [128, D], F32, kind="ExternalOutput").ap()
        dbg_yb = ctx.dram("dbg_yb", [128, D], BF16, kind="ExternalOutput").ap()
        dbg_mt = ctx.dram("dbg_mt", [128, D], BF16, kind="ExternalOutput").ap()

    sems = _basic_sems(p)
    s_const, s_pe = sems["const"], sems["pe"]
    s_v, s_s, s_g = sems["v"], sems["s"], sems["g"]

    ident = ctx.sb("identsb", [128, 128], BF16)
    identf = ctx.sb("identfsb", [128, 128], F32)
    onehot = ctx.sb("onehotsb", [16, D], F32)
    ones1 = ctx.sb("ones1", [1, 128], F32)
    gamT = ctx.sb("gamT", [16, ntok], F32)
    carry_sb = ctx.sb("carry_sb", [128, 16], F32)
    mnw_b = ctx.sb("mnw_b", [128, D], F32)
    mnb_b = ctx.sb("mnb_b", [128, D], F32)
    gnw_b = ctx.sb("gnw_b", [128, D], F32)
    gnb_b = ctx.sb("gnb_b", [128, D], F32)
    prows = ctx.sb("prows", [1, 4 * D], F32)

    pool_mm = PsumPool(ctx, "pm", 2)
    pool_st = PsumPool(ctx, "pst", 2)
    pool_tf = PsumPool(ctx, "ptf", 1, (128, 128), F32)
    pool_t = PsumPool(ctx, "pt", 2, (128, 128), BF16)

    for dst, srcd in ((ident, identd), (identf, identfd), (onehot, onehotd),
                      (gamT, gamT_in), (carry_sb, carry)):
        p.sync.dma_start(out=dst[:, :], in_=srcd).inc(s_const, 16)
    for j, srcd in enumerate((mn_w, mn_b, gn_w, gn_b)):
        p.sync.dma_start(out=prows[:, j * D:(j + 1) * D], in_=srcd
                         ).inc(s_const, 16)
    p.gpsimd.memset(ones1[:, :], 1.0).inc(s_g, 1)
    const_g = s_g.n

    p.mark("a2_consts")
    p.tensor.wait(s_const)
    p.tensor.wait(s_g, const_g)
    for j, dst in enumerate((mnw_b, mnb_b, gnw_b, gnb_b)):
        for c0 in range(0, D, 512):
            ps, slot = pool_mm.acquire(p.tensor)
            p.tensor.matmul(ps[:, 0:512], ones1[:, 0:128],
                            prows[:, j * D + c0:j * D + c0 + 512],
                            start=True, stop=True).inc(s_pe)
            p.vector.wait(s_pe)
            p.vector.tensor_copy(dst[:, c0:c0 + 512], ps[:, 0:512]).inc(s_v)
            pool_mm.release(slot, s_v)

    p.mark("a2_bcast")
    gb = ctx.sb("gb", [128, ntok], F32)
    btTt = ctx.sb("btTt", [128, ntok], BF16)
    mem_h = ctx.sb("mem_h", [128, ntok], BF16)
    btT = DmaBuf(p, "btT", btTt)
    memst = DmaBuf(p, "memst", mem_h)
    for h in range(16):
        p.tensor.wait(s_v)
        for c0 in range(0, ntok, 512):
            psg, gslot = pool_mm.acquire(p.tensor)
            p.tensor.matmul(psg[:, 0:512], onehot[0:16, h * 128:(h + 1) * 128],
                            gamT[0:16, c0:c0 + 512], start=True, stop=True
                            ).inc(s_pe)
            p.vector.wait(s_pe)
            p.vector.tensor_copy(gb[:, c0:c0 + 512], psg[:, 0:512]).inc(s_v)
            pool_mm.release(gslot, s_v)
        btT.load(p.sync, bt_scr[:, h * 128:(h + 1) * 128], transpose=True)
        btT.wait(p.vector)
        memst.wait(p.vector)
        p.vector.tensor_tensor_scan(mem_h[:, :], gb[:, :], btTt[:, :],
                                    carry_sb[:, h:h + 1], ALU.mult, ALU.add)
        p.vector.finc(s_v)
        btT.consumed(s_v)
        p.sync.wait(s_v)
        memst.store(p.sync, mem_scr[h * 128:(h + 1) * 128, :])

    p.mark("a2_scan")
    y_fm = ctx.sb("y_fm", [128, 16, ntok], BF16)
    mem_tm = DmaBuf(p, "mem_tm", ctx.sb("mem_tm", [128, D], BF16))
    qn_t = DmaBuf(p, "qn_t", ctx.sb("qn_t", [128, D], BF16))
    og_t = DmaBuf(p, "og_t", ctx.sb("og_t", [128, D], BF16))
    sqs = ctx.sb("sqs", [128, D], F32)
    t2 = ctx.sb("t2", [128, D], F32)
    u2 = ctx.sb("u2", [128, D], F32)
    yb = ctx.sb("yb", [128, D], BF16)
    st1 = ctx.sb("st1", [128, 16], F32)
    st2 = ctx.sb("st2", [128, 16], F32)
    st3 = ctx.sb("st3", [128, 16], F32)
    st4 = ctx.sb("st4", [128, 16], F32)
    stT = ctx.sb("stT", [16, 128], F32)
    stT2 = ctx.sb("stT2", [16, 128], F32)
    EPS_GN = 1e-5

    def group_stats(src):
        p.vector.reduce_sum(st1.ap().unsqueeze(2),
                            src.ap().rearrange("p (h d) -> p h d", d=128),
                            axis=AX.X)
        p.vector.barrier(s_v)
        p.scalar.wait(s_v)
        p.scalar.activation(sqs[:, :], src[:, :], AF.Square).inc(s_s)
        p.vector.wait(s_s)
        p.vector.reduce_sum(st2.ap().unsqueeze(2),
                            sqs.ap().rearrange("p (h d) -> p h d", d=128),
                            axis=AX.X)
        p.vector.barrier(s_v)
        p.vector.tensor_scalar(st1[:, :], st1[:, :], 1.0 / 128, None, ALU.mult)
        p.vector.barrier(s_v)
        p.vector.tensor_mul(st3[:, :], st1[:, :], st1[:, :])
        p.vector.barrier(s_v)
        p.vector.scalar_tensor_tensor(st3[:, :], st2[:, :], 1.0 / 128,
                                      st3[:, :], ALU.mult, ALU.subtract)
        p.vector.barrier(s_v)
        p.vector.tensor_scalar(st3[:, :], st3[:, :], EPS_GN, None, ALU.add)
        p.vector.barrier(s_v)
        p.vector.reciprocal(st3[:, :], st3[:, :])
        p.vector.finc(s_v)
        p.scalar.wait(s_v)
        p.scalar.activation(st4[:, :], st3[:, :], AF.Sqrt).inc(s_s)
        p.tensor.wait(s_s)
        for stsrc, stdst in ((st1, stT), (st4, stT2)):
            pst, sl = pool_tf.acquire(p.tensor)
            p.tensor.transpose(pst[0:16, 0:128], stsrc[:, 0:16], identf[:, :]
                               ).inc(s_pe)
            p.vector.wait(s_pe)
            p.vector.tensor_copy(stdst[0:16, :], pst[0:16, 0:128]).inc(s_v)
            pool_tf.release(sl, s_v)

    def apply_norm(src, wb, bbias, extra_mul, out):
        p.tensor.wait(s_v)
        for c0 in range(0, D, 512):
            psM, slM = pool_st.acquire(p.tensor)
            p.tensor.matmul(psM[:, 0:512], stT[0:16, 0:128],
                            onehot[0:16, c0:c0 + 512], start=True, stop=True)
            psR, slR = pool_st.acquire(p.tensor)
            p.tensor.matmul(psR[:, 0:512], stT2[0:16, 0:128],
                            onehot[0:16, c0:c0 + 512], start=True, stop=True
                            ).inc(s_pe)
            p.vector.wait(s_pe)
            p.vector.tensor_sub(u2[:, c0:c0 + 512], src[:, c0:c0 + 512],
                                psM[:, 0:512])
            p.vector.tensor_mul(u2[:, c0:c0 + 512], u2[:, c0:c0 + 512],
                                psR[:, 0:512]).inc(s_v)
            pool_st.release(slM, s_v)
            pool_st.release(slR, s_v)
            p.vector.tensor_mul(u2[:, c0:c0 + 512], u2[:, c0:c0 + 512],
                                wb[:, c0:c0 + 512])
            p.vector.tensor_add(u2[:, c0:c0 + 512], u2[:, c0:c0 + 512],
                                bbias[:, c0:c0 + 512])
            p.vector.tensor_mul(out[:, c0:c0 + 512], u2[:, c0:c0 + 512],
                                extra_mul[:, c0:c0 + 512]).inc(s_g)

    for i in range(ntile):
        p.mark(f"t{i}_ld")
        for buf, view, tr in ((mem_tm, mem_scr[:, i * 128:(i + 1) * 128], True),
                              (qn_t, qn_scr[i * 128:(i + 1) * 128, :], False),
                              (og_t, og_scr[i * 128:(i + 1) * 128, :], False)):
            if i == 0 and tr:
                memst.wait(p.sync)
            buf.load(p.sync, view, transpose=tr)
        mem_tm.wait(p.vector)
        qn_t.wait(p.vector)
        og_t.wait(p.vector)
        qn_t.wait(p.gpsimd)
        og_t.wait(p.gpsimd)
        p.mark(f"t{i}_gs1")
        group_stats(mem_tm.t)
        if DEBUG_L0A2 and i == 0:
            db1 = DmaBuf(p, "dbg1", None)
            p.sync.wait(s_v)
            db1.store(p.sync, dbg_st[0], src=stT[0:16, :])
            db1.store(p.sync, dbg_st[1], src=stT2[0:16, :])
            db1.store(p.sync, dbg_mt, src=mem_tm.t[:, :])
        p.mark(f"t{i}_an1")
        apply_norm(mem_tm.t, mnw_b, mnb_b, qn_t.t, t2)
        mem_tm.consumed(s_v)
        qn_t.consumed(s_g)
        p.vector.wait(s_g)
        p.mark(f"t{i}_gs2")
        if DEBUG_L0A2 and i == 0:
            p.sync.wait(s_g)
            db1.store(p.sync, dbg_t2, src=t2[:, :])
        group_stats(t2)
        p.mark(f"t{i}_an2")
        apply_norm(t2, gnw_b, gnb_b, og_t.t, yb)
        og_t.consumed(s_g)
        p.mark(f"t{i}_tr")
        if DEBUG_L0A2 and i == 0:
            p.sync.wait(s_g)
            db1.store(p.sync, dbg_yb, src=yb[:, :])
        p.tensor.wait(s_g)
        for c in range(16):
            ps, slot = pool_t.acquire(p.tensor)
            p.tensor.transpose(ps[:, 0:128], yb[:, c * 128:(c + 1) * 128],
                               ident[:, :]).inc(s_pe)
            p.vector.wait(s_pe)
            p.vector.tensor_copy(y_fm[:, c, i * 128:(i + 1) * 128],
                                 ps[:, 0:128]).inc(s_v)
            pool_t.release(slot, s_v)

    p.mark("a2_tiles")
    wo_pool = WBufPool(ctx, "wo_b", 16, 512, 2)
    xr = DmaBuf(p, "xr", ctx.sb("xr", [128, 512], F32))
    ob = [DmaBuf(p, f"ob{k}", ctx.sb(f"ob{k}", [128, 512], F32))
          for k in range(2)]

    def epi_wo(i, c0, nw, ps, slot, ppool):
        kslot = i % 2
        xr.load(p.sync, x[i * 128:(i + 1) * 128, c0:c0 + nw])
        xr.wait(p.vector)
        p.vector.wait(s_pe)
        if ob[kslot].guard is not None:
            p.vector.wait(*ob[kslot].guard)
            ob[kslot].guard = None
        p.vector.tensor_add(ob[kslot].t[:, 0:nw], ps[:, 0:nw], xr.t[:, 0:nw]
                            ).inc(s_v)
        xr.consumed(s_v)
        ppool.release(slot, s_v)
        p.sync.wait(s_v)
        ob[kslot].store(p.sync, xnew[i * 128:(i + 1) * 128, c0:c0 + nw])
        ob[kslot].stored()

    emit_proj(ctx, y_fm, 0, ntile, woT, D, D, sems,
              {"w": wo_pool, "ps": pool_st}, epi_wo)

    for bq in ob:
        bq.wait(p.sync)
    p.flush()
    es.close()
    return nc


_IDENT = make_identity_bf16()
_IDENTF = np.eye(128, dtype=np.float32)
_ONEHOT = make_blockones()


def _core_slices(c):
    b, half = c // 2, c % 2
    return b, slice(half * NTOK, (half + 1) * NTOK), half


def run_l0a(x_full, w):
    nc1 = cached_build("l0a1", build_l0a1)
    wqT = as_np(np.asarray(w["L0_attn_Wq"]).T)
    wkT = as_np(np.asarray(w["L0_attn_Wk"]).T)
    wvT = as_np(np.asarray(w["L0_attn_Wv"]).T)
    wigT = as_np(np.asarray(w["L0_attn_Wig"]).T)
    wogT = as_np(np.asarray(w["L0_attn_Wog"]).T)
    wgT = as_np(np.asarray(w["L0_attn_Wg"]).T)
    woT = as_np(np.asarray(w["L0_attn_Wo"]).T)
    conv_w = as_np(w["L0_attn_conv_w"])
    conv_bT = as_np(np.asarray(w["L0_attn_conv_b"], np.float32
                               ).reshape(16, 128).T)
    b_ig = as_np(w["L0_attn_b_ig"]).reshape(1, D)
    b_og = as_np(w["L0_attn_b_og"]).reshape(1, D)
    b_g = as_np(w["L0_attn_b_g"]).reshape(1, 16)
    zeros_halo = np.zeros((3, D), np.float32)
    in_maps = []
    for c in range(NCORES):
        b, rows, half = _core_slices(c)
        xh = as_np(x_full[b, rows])
        halo = as_np(x_full[b, NTOK - 3:NTOK]) if half == 1 else zeros_halo
        in_maps.append(dict(x=xh, halo=halo, wqT=wqT, wkT=wkT, wvT=wvT,
                            wigT=wigT, wogT=wogT, wgT=wgT,
                            b_ig=b_ig, b_og=b_og, b_g=b_g,
                            conv_w=conv_w, conv_bT=conv_bT, onehot=_ONEHOT))
    res1 = run_bass_kernel_spmd(nc1, in_maps, core_ids=list(range(NCORES)))

    nc2 = cached_build("l0a2", build_l0a2)
    mn_w = as_np(w["L0_attn_mn_w"]).reshape(1, D)
    mn_b = as_np(w["L0_attn_mn_b"]).reshape(1, D)
    gn_w = as_np(w["L0_attn_gn_w"]).reshape(1, D)
    gn_b = as_np(w["L0_attn_gn_b"]).reshape(1, D)
    zero_carry = np.zeros((128, 16), np.float32)
    in_maps2 = []
    for c in range(NCORES):
        b, rows, half = _core_slices(c)
        r1 = res1.results[c]
        carry = (np.ascontiguousarray(np.asarray(res1.results[c - 1]["last_state"],
                                                 np.float32))
                 if half == 1 else zero_carry)
        in_maps2.append(dict(x=as_np(x_full[b, rows]),
                             qn_scr=np.ascontiguousarray(r1["qn_scr"]),
                             og_scr=np.ascontiguousarray(r1["og_scr"]),
                             bt_scr=np.ascontiguousarray(r1["bt_scr"]),
                             gamT_in=np.ascontiguousarray(r1["gamT_out"]),
                             carry=carry, woT=woT,
                             mn_w=mn_w, mn_b=mn_b, gn_w=gn_w, gn_b=gn_b,
                             ident=_IDENT, identf=_IDENTF, onehot=_ONEHOT))
    res2 = run_bass_kernel_spmd(nc2, in_maps2, core_ids=list(range(NCORES)))
    out = np.empty((B, T, D), np.float32)
    for c in range(NCORES):
        b, rows, half = _core_slices(c)
        out[b, rows] = np.asarray(res2.results[c]["xnew"], np.float32)
    return out


def make_epi_residual(ctx, p, sems, x_dram, xnew_dram, extra_psum_cb=None):
    s_v = sems["v"]
    xr = DmaBuf(p, "xr_res", ctx.sb("xr_res", [128, 512], F32))
    ob = [DmaBuf(p, f"ob_res{k}", ctx.sb(f"ob_res{k}", [128, 512], F32))
          for k in range(2)]

    def epi(i, c0, nw, ps, slot, ppool):
        kslot = i % 2
        xr.load(p.sync, x_dram[i * 128:(i + 1) * 128, c0:c0 + nw],
                out=xr.t[:, 0:nw])
        xr.wait(p.vector)
        p.vector.wait(sems["pe"])
        if ob[kslot].guard is not None:
            p.vector.wait(*ob[kslot].guard)
            ob[kslot].guard = None
        if extra_psum_cb is not None:
            extra_psum_cb(i, c0, nw, ps)
        p.vector.tensor_add(ob[kslot].t[:, 0:nw], ps[:, 0:nw], xr.t[:, 0:nw]
                            ).inc(s_v)
        xr.consumed(s_v)
        ppool.release(slot, s_v)
        p.sync.wait(s_v)
        ob[kslot].store(p.sync, xnew_dram[i * 128:(i + 1) * 128, c0:c0 + nw])
        ob[kslot].stored()
    return epi, ob


def make_epi_store_bf16(ctx, p, sems, dst_dram, name, act=None, scale=1.0):
    s_s = sems["s"]
    bufs = [DmaBuf(p, f"{name}{k}", ctx.sb(f"{name}{k}", [128, 512], BF16))
            for k in range(2)]

    def epi(i, c0, nw, ps, slot, ppool):
        kslot = i % 2
        p.scalar.wait(sems["pe"])
        if bufs[kslot].guard is not None:
            p.scalar.wait(*bufs[kslot].guard)
            bufs[kslot].guard = None
        p.scalar.activation(bufs[kslot].t[:, 0:nw], ps[:, 0:nw],
                            act or AF.Copy, scale=scale).inc(s_s)
        ppool.release(slot, s_s)
        p.sync.wait(s_s)
        bufs[kslot].store(p.sync, dst_dram[i * 128:(i + 1) * 128, c0:c0 + nw],
                          src=bufs[kslot].t[:, 0:nw])
        bufs[kslot].stored()
    return epi, bufs


def build_l1f():
    nc, es, p, ctx = _common_ctx()
    ntile, ntok = NT8, NTOK
    x = ctx.dram("x", [ntok, D], F32, kind="ExternalInput").ap()
    wT = ctx.dram("wT", [D, D], F32, kind="ExternalInput").ap()
    bias = ctx.dram("bias", [1, D], F32, kind="ExternalInput").ap()
    gwT = ctx.dram("gwT", [D, 8], F32, kind="ExternalInput").ap()
    gb = ctx.dram("gb", [1, 8], F32, kind="ExternalInput").ap()
    aT = ctx.dram("aT", [D, 128], F32, kind="ExternalInput").ap()
    bT = ctx.dram("bT", [128, D], F32, kind="ExternalInput").ap()
    identd = ctx.dram("ident", [128, 128], BF16, kind="ExternalInput").ap()
    xnew = ctx.dram("xnew", [ntok, D], F32, kind="ExternalOutput").ap()
    xln_scr = ctx.dram("xln_scr", [ntok, D], BF16, kind="Internal").ap()

    sems = _basic_sems(p)
    s_const, s_pe, s_v, s_s, s_g = (sems["const"], sems["pe"], sems["v"],
                                    sems["s"], sems["g"])

    x_fm = ctx.sb("x_fm", [128, 16, ntok], BF16)
    ident = ctx.sb("identsb", [128, 128], BF16)
    ones1 = ctx.sb("ones1", [1, 128], F32)
    bias_sb = ctx.sb("bias_sb", [1, D], F32)
    gb_sb = ctx.sb("gb_sb", [1, 8], F32)
    gates = ctx.sb("gates", [128, 8, 8], F32)
    gsc = ctx.sb("gsc", [128, 16], F32)
    xasT = ctx.sb("xasT", [128, 8, 128], BF16)
    xas = ctx.sb("xas", [128, 128], BF16)
    pool_mm = PsumPool(ctx, "pm", 3)
    pool_t = PsumPool(ctx, "pt", 2, (128, 128), BF16)

    p.sync.dma_start(out=ident[:, :], in_=identd).inc(s_const, 16)
    p.sync.dma_start(out=bias_sb[:, :], in_=bias).inc(s_const, 16)
    p.sync.dma_start(out=gb_sb[:, :], in_=gb).inc(s_const, 16)
    p.gpsimd.memset(ones1[:, :], 1.0).inc(s_g, 1)
    const_g = s_g.n

    xfmld = emit_ln_prelude(ctx, x, ntile, x_fm, xln_scr, sems)
    xfmld.wait(p.tensor)
    p.tensor.wait(s_const)
    p.tensor.wait(s_g, const_g)

    wg_pool = WBufPool(ctx, "wg_b", 16, 512, 1)

    def epi_gate(i, c0, nw, ps, slot, ppool):
        p.vector.wait(s_pe)
        p.vector.tensor_copy(gsc[:, 0:8], ps[:, 0:8]).inc(s_v)
        ppool.release(slot, s_v)
        p.vector.barrier(s_v)
        p.vector.reduce_max(gsc[:, 8:9], gsc[:, 0:8], axis=AX.X)
        p.vector.barrier(s_v)
        p.vector.tensor_scalar(gsc[:, 9:10], gsc[:, 8:9], -1.0, None, ALU.mult)
        p.vector.barrier(s_v)
        p.vector.tensor_scalar(gates[:, i, 0:8], gsc[:, 0:8], gsc[:, 8:9],
                               None, ALU.is_ge)
        p.vector.barrier(s_v)
        p.vector.scalar_tensor_tensor(gsc[:, 10:18] if False else gates[:, i, 0:8],
                                      gates[:, i, 0:8], -1e30, gsc[:, 0:8],
                                      ALU.mult, ALU.add)
        p.vector.barrier(s_v)
        p.vector.reduce_max(gsc[:, 10:11], gates[:, i, 0:8], axis=AX.X)
        p.vector.barrier(s_v)
        p.vector.tensor_scalar(gates[:, i, 0:8], gsc[:, 0:8], gsc[:, 10:11],
                               None, ALU.is_ge)
        p.vector.barrier(s_v)
        p.scalar.wait(s_v)
        p.scalar.activation(gsc[:, 11:19] if False else gsc[:, 0:8],
                            gsc[:, 0:8], AF.Exp, bias=gsc[:, 9:10]).inc(s_s)
        p.vector.wait(s_s)
        p.vector.tensor_mul(gates[:, i, 0:8], gates[:, i, 0:8], gsc[:, 0:8])
        p.vector.barrier(s_v)
        p.vector.reduce_sum(gsc[:, 12:13], gates[:, i, 0:8], axis=AX.X)
        p.vector.barrier(s_v)
        p.vector.reciprocal(gsc[:, 13:14], gsc[:, 12:13])
        p.vector.barrier(s_v)
        p.vector.tensor_scalar(gates[:, i, 0:8], gates[:, i, 0:8],
                               gsc[:, 13:14], None, ALU.mult).inc(s_v)

    emit_proj(ctx, x_fm, 0, ntile, gwT, D, 8, sems,
              {"w": wg_pool, "ps": pool_mm}, epi_gate,
              bias_row=gb_sb, ones1=ones1)

    def epi_xa(i, c0, nw, ps, slot, ppool):
        p.vector.wait(s_pe)
        for e in range(8):
            p.vector.tensor_scalar(xas[:, e * 16:(e + 1) * 16],
                                   ps[:, e * 16:(e + 1) * 16],
                                   gates[:, i, e:e + 1], None, ALU.mult)
        p.vector.finc(s_v)
        ppool.release(slot, s_v)
        p.tensor.wait(s_v)
        pst, sl = pool_t.acquire(p.tensor)
        p.tensor.transpose(pst[:, :], xas[:, :], ident[:, :]).inc(s_pe)
        p.vector.wait(s_pe)
        p.vector.tensor_copy(xasT[:, i, :], pst[:, :]).inc(s_v)
        pool_t.release(sl, s_v)

    emit_proj(ctx, x_fm, 0, ntile, aT, D, 128, sems,
              {"w": wg_pool, "ps": pool_mm}, epi_xa)

    epi_res, obufs = make_epi_residual(ctx, p, sems, x, xnew)
    w_pool = WBufPool(ctx, "w_b", 16, 512, 2)
    btsb = DmaBuf(p, "btsb", ctx.sb("btsb", [128, 512], BF16))
    p.tensor.wait(s_v)
    for c0 in range(0, D, 512):
        wb = w_pool.load(p, wT_chunk_view(wT, D, c0, 512), s_pe)
        btsb.load(p.gpsimd, bT[:, c0:c0 + 512])
        wb.wait(p.tensor)
        btsb.wait(p.tensor)
        for i in range(ntile):
            ps, slot = pool_mm.acquire(p.tensor)
            for c in range(16):
                p.tensor.matmul(ps[:, 0:512], x_fm[:, c, i * 128:(i + 1) * 128],
                                wb.t[:, c, 0:512], start=(c == 0), stop=False)
            p.tensor.matmul(ps[:, 0:512], xasT[:, i, :], btsb.t[:, 0:512],
                            start=False, stop=False)
            p.tensor.matmul(ps[:, 0:512], ones1[:, 0:128],
                            bias_sb[:, c0:c0 + 512], start=False, stop=True
                            ).inc(s_pe)
            epi_res(i, c0, 512, ps, slot, pool_mm)
        wb.consumed(s_pe)
        btsb.consumed(s_pe)
    for bq in obufs:
        bq.wait(p.sync)
    p.flush()
    es.close()
    return nc


def emit_mxfp4_quant(ctx, p, sems, src, out_bf, nb, work):
    s_v = sems["v"]
    N = nb * 32
    am, rs, sc = work["am"], work["rs"], work["sc"]
    t, c2, c4, lsb = work["t"], work["c2"], work["c4"], work["lsb"]
    v = p.vector
    v.reduce_max(am.ap()[:, 0:nb].unsqueeze(2),
                 src.rearrange("p (b e) -> p b e", e=32), axis=AX.X,
                 apply_absolute_value=True)
    v.barrier(s_v)
    v.tensor_scalar(rs.ap()[:, 0:nb].bitcast(U32), am.ap()[:, 0:nb].bitcast(U32),
                    0x7F800000, None, ALU.bitwise_and)
    v.barrier(s_v)
    v.tensor_copy(am[:, 0:nb], rs.ap()[:, 0:nb].bitcast(U32))
    v.barrier(s_v)
    v.tensor_scalar(sc[:, 0:nb], am[:, 0:nb], float(2 << 23), None,
                    ALU.subtract)
    v.tensor_scalar(am[:, 0:nb], am[:, 0:nb], -1.0, float(256 << 23),
                    ALU.mult, ALU.add)
    v.finc(s_v)
    v.barrier(s_v)
    v.tensor_copy(sc.ap()[:, 0:nb].bitcast(U32), sc[:, 0:nb])
    v.tensor_copy(rs.ap()[:, 0:nb].bitcast(U32), am[:, 0:nb])
    v.finc(s_v)
    v.barrier(s_v)
    for b in range(nb):
        v.tensor_scalar(t[:, b * 32:(b + 1) * 32], src[:, b * 32:(b + 1) * 32],
                        rs[:, b:b + 1], None, ALU.mult)
    v.finc(s_v)
    v.tensor_scalar(t[:, 0:N], t[:, 0:N], 6.0, None, ALU.min)
    v.finc(s_v)
    v.tensor_scalar(t[:, 0:N], t[:, 0:N], -6.0, None, ALU.max)
    v.finc(s_v)
    v.tensor_mul(c2[:, 0:N], t[:, 0:N], t[:, 0:N])
    v.finc(s_v)
    v.tensor_scalar(c4[:, 0:N], c2[:, 0:N], 16.0, None, ALU.is_ge)
    v.tensor_scalar(c2[:, 0:N], c2[:, 0:N], 4.0, None, ALU.is_ge)
    v.finc(s_v)
    v.tensor_scalar(lsb[:, 0:N], c2[:, 0:N], 0.5, 0.5, ALU.mult, ALU.add)
    v.tensor_scalar(c2[:, 0:N], c2[:, 0:N], -1.0, 2.0, ALU.mult, ALU.add)
    v.finc(s_v)
    v.tensor_add(lsb[:, 0:N], lsb[:, 0:N], c4[:, 0:N])
    v.scalar_tensor_tensor(c4[:, 0:N], c4[:, 0:N], -0.5, c2[:, 0:N],
                           ALU.mult, ALU.add)
    v.finc(s_v)
    v.tensor_mul(t[:, 0:N], t[:, 0:N], c4[:, 0:N])
    v.finc(s_v)
    v.tensor_scalar(t[:, 0:N], t[:, 0:N], MAGIC, None, ALU.add)
    v.finc(s_v)
    v.tensor_scalar(t[:, 0:N], t[:, 0:N], MAGIC, None, ALU.subtract)
    v.finc(s_v)
    v.tensor_mul(t[:, 0:N], t[:, 0:N], lsb[:, 0:N])
    v.finc(s_v)
    for b in range(nb):
        v.tensor_scalar(out_bf[:, b * 32:(b + 1) * 32], t[:, b * 32:(b + 1) * 32],
                        sc[:, b:b + 1], None, ALU.mult)
    v.finc(s_v)


def build_l0f():
    nc, es, p, ctx = _common_ctx()
    ntile, ntok = NT8, NTOK
    x = ctx.dram("x", [ntok, D], F32, kind="ExternalInput").ap()
    w_raw = ctx.dram("w_raw", [D, D], F32, kind="ExternalInput").ap()
    bias = ctx.dram("bias", [1, D], F32, kind="ExternalInput").ap()
    xnew = ctx.dram("xnew", [ntok, D], F32, kind="ExternalOutput").ap()
    xq_scr = ctx.dram("xq_scr", [ntok, D], BF16, kind="Internal").ap()
    wq_scr = ctx.dram("wq_scr", [D, D], BF16, kind="Internal").ap()

    sems = _basic_sems(p)
    s_const, s_pe, s_v, s_s, s_g = (sems["const"], sems["pe"], sems["v"],
                                    sems["s"], sems["g"])
    ones1 = ctx.sb("ones1", [1, 128], F32)
    bias_sb = ctx.sb("bias_sb", [1, D], F32)
    p.sync.dma_start(out=bias_sb[:, :], in_=bias).inc(s_const, 16)
    p.gpsimd.memset(ones1[:, :], 1.0).inc(s_g, 1)
    const_g = s_g.n

    x_fm = ctx.sb("x_fm", [128, 16, ntok], BF16)
    work = dict(am=ctx.sb("qam", [128, 64], F32),
                rs=ctx.sb("qrs", [128, 64], F32),
                sc=ctx.sb("qsc", [128, 64], F32),
                t=ctx.sb("qt", [128, D], F32),
                c2=ctx.sb("qc2", [128, D], F32),
                c4=ctx.sb("qc4", [128, D], F32),
                lsb=ctx.sb("qlsb", [128, D], F32))
    outb = DmaBuf(p, "outb", ctx.sb("outb", [128, D], BF16))
    xt = DmaBuf(p, "lnx", ctx.sb("lnx", [128, D], F32))
    xc = ctx.sb("lnc", [128, D], F32)
    stat = ctx.sb("lnstat", [128, 8], F32)
    for i in range(ntile):
        xt.load(p.sync, x[i * 128:(i + 1) * 128, :])
        xt.wait(p.vector)
        p.vector.reduce_sum(stat[:, 0:1], xt.t[:, :], axis=AX.X)
        p.vector.barrier(s_v)
        p.vector.tensor_scalar(stat[:, 1:2], stat[:, 0:1], 1.0 / D, None,
                               ALU.mult)
        p.vector.barrier(s_v)
        p.vector.tensor_scalar(xc[:, :], xt.t[:, :], stat[:, 1:2], None,
                               ALU.subtract)
        p.vector.finc(s_v)
        p.scalar.wait(s_v)
        p.scalar.activation(xt.t[:, :], xc[:, :], AF.Square,
                            accum_out=stat[:, 2:3])
        p.scalar.finc(s_s)
        xt.consumed(s_s)
        p.vector.wait(s_s)
        p.vector.tensor_scalar(stat[:, 3:4], stat[:, 2:3], 1.0 / D, 1e-5,
                               ALU.mult, ALU.add)
        p.vector.barrier(s_v)
        p.vector.reciprocal(stat[:, 4:5], stat[:, 3:4])
        p.vector.finc(s_v)
        p.scalar.wait(s_v)
        p.scalar.activation(stat[:, 5:6], stat[:, 4:5], AF.Sqrt)
        p.scalar.finc(s_s)
        p.vector.wait(s_s)
        p.vector.tensor_scalar(xc[:, :], xc[:, :], stat[:, 5:6], None,
                               ALU.mult)
        p.vector.finc(s_v)
        outb.wait(p.vector)
        emit_mxfp4_quant(ctx, p, sems, xc[:, :].opt(), outb.t[:, :], 64, work)
        p.sync.wait(s_v)
        outb.store(p.sync, xq_scr[i * 128:(i + 1) * 128, :])
        outb.stored()
    srcf = DmaBuf(p, "srcf", ctx.sb("srcf", [128, D], F32))
    for i in range(16):
        srcf.load(p.sync, w_raw[i * 128:(i + 1) * 128, :])
        srcf.wait(p.vector)
        outb.wait(p.vector)
        emit_mxfp4_quant(ctx, p, sems, srcf.t[:, :].opt(), outb.t[:, :],
                         64, work)
        srcf.consumed(s_v)
        p.sync.wait(s_v)
        outb.store(p.sync, wq_scr[i * 128:(i + 1) * 128, :])
        outb.stored()

    outb.wait(p.sync)
    xqld = DmaBuf(p, "xqld", None)
    for c in range(16):
        xqld.load(p.sync, xq_scr[:, c * 128:(c + 1) * 128],
                  out=x_fm[:, c, :], transpose=True)

    epi_res, obufs = make_epi_residual(ctx, p, sems, x, xnew)
    wq_fm = [DmaBuf(p, f"wqfm{k}", ctx.sb(f"wqfm{k}", [128, 16, 512], BF16))
             for k in range(2)]
    pool_mm = PsumPool(ctx, "pm", 2)
    xqld.wait(p.tensor)
    p.tensor.wait(s_const)
    p.tensor.wait(s_g, const_g)
    for j, c0 in enumerate(range(0, D, 512)):
        wb = wq_fm[j % 2]
        for c in range(16):
            wb.load(p.sync, wq_scr[c0:c0 + 512, c * 128:(c + 1) * 128],
                    out=wb.t[:, c, :], transpose=True)
        wb.wait(p.tensor)
        for i in range(ntile):
            ps, slot = pool_mm.acquire(p.tensor)
            for c in range(16):
                p.tensor.matmul(ps[:, 0:512], x_fm[:, c, i * 128:(i + 1) * 128],
                                wb.t[:, c, 0:512], start=(c == 0), stop=False)
            p.tensor.matmul(ps[:, 0:512], ones1[:, 0:128],
                            bias_sb[:, c0:c0 + 512], start=False, stop=True
                            ).inc(s_pe)
            epi_res(i, c0, 512, ps, slot, pool_mm)
        wb.consumed(s_pe)
    for bq in obufs:
        bq.wait(p.sync)
    p.flush()
    es.close()
    return nc


def emit_ln_f32_tile(ctx, p, sems, bufs, x_dram, i, out_cb):
    s_v, s_s = sems["v"], sems["s"]
    xt, xc, stat = bufs["xt"], bufs["xc"], bufs["stat"]
    xt.load(p.sync, x_dram[i * 128:(i + 1) * 128, :])
    xt.wait(p.vector)
    p.vector.reduce_sum(stat[:, 0:1], xt.t[:, :], axis=AX.X)
    p.vector.barrier(s_v)
    p.vector.tensor_scalar(stat[:, 1:2], stat[:, 0:1], 1.0 / D, None, ALU.mult)
    p.vector.barrier(s_v)
    p.vector.tensor_scalar(xc[:, :], xt.t[:, :], stat[:, 1:2], None,
                           ALU.subtract)
    p.vector.finc(s_v)
    p.scalar.wait(s_v)
    p.scalar.activation(xt.t[:, :], xc[:, :], AF.Square, accum_out=stat[:, 2:3])
    p.scalar.finc(s_s)
    xt.consumed(s_s)
    p.vector.wait(s_s)
    p.vector.tensor_scalar(stat[:, 3:4], stat[:, 2:3], 1.0 / D, 1e-5,
                           ALU.mult, ALU.add)
    p.vector.barrier(s_v)
    p.vector.reciprocal(stat[:, 4:5], stat[:, 3:4])
    p.vector.finc(s_v)
    p.scalar.wait(s_v)
    p.scalar.activation(stat[:, 5:6], stat[:, 4:5], AF.Sqrt)
    p.scalar.finc(s_s)
    p.vector.wait(s_s)
    p.vector.tensor_scalar(xc[:, :], xc[:, :], stat[:, 5:6], None, ALU.mult)
    p.vector.finc(s_v)
    out_cb()


def emit_q8_tile(ctx, p, sems, src_f32, n, outq_bf, amc_dst, eps=1e-5):
    s_v, s_s = sems["v"], sems["s"]
    st = ctx.sb_q8 if hasattr(ctx, "sb_q8") else None
    if st is None:
        st = ctx.sb("q8st", [128, 4], F32)
        ctx.sb_q8 = st
    p.vector.reduce_max(st[:, 0:1], src_f32[:, 0:n], axis=AX.X,
                        apply_absolute_value=True)
    p.vector.barrier(s_v)
    p.vector.tensor_scalar(amc_dst, st[:, 0:1], eps, None, ALU.max)
    p.vector.barrier(s_v)
    p.vector.reciprocal(st[:, 1:2], amc_dst)
    p.vector.barrier(s_v)
    p.vector.tensor_scalar(st[:, 2:3], st[:, 1:2], 127.0, None, ALU.mult)
    p.vector.barrier(s_v)
    p.vector.tensor_scalar(src_f32[:, 0:n], src_f32[:, 0:n], st[:, 2:3],
                           MAGIC, ALU.mult, ALU.add)
    p.vector.finc(s_v)
    p.vector.tensor_scalar(outq_bf[:, 0:n], src_f32[:, 0:n], MAGIC, None,
                           ALU.subtract)
    p.vector.finc(s_v)


def emit_gamma(ctx, p, sems, wT, k_dim, n_dim, gcol, gname, clip_eps=1e-5):
    s_v, s_pe, s_g = sems["v"], sems["pe"], sems["g"]
    if not hasattr(ctx, "gma_bufs"):
        ctx.gma_bufs = dict(
            wld=DmaBuf(p, "gmawld", ctx.gma_wld_t),
            acc=ctx.sb("gmaacc", [128, 2], F32),
            cell=ctx.sb("gmacell", [1, 2], F32),
            ones_col=ctx.sb("gmaones", [128, 1], F32),
        )
        p.gpsimd.memset(ctx.gma_bufs["ones_col"][:, :], 1.0).inc(s_g)
    b = ctx.gma_bufs
    p.vector.wait(s_g)
    p.vector.memset(b["acc"][:, :], 0.0)
    p.vector.finc(s_v)
    kc = k_dim // 128
    for kb0 in range(0, kc, 16):
        kbn = min(16, kc - kb0)
        for c0 in range(0, n_dim, 512):
            nw = min(512, n_dim - c0)
            b["wld"].load(p.sync, wT_chunk_view(wT, k_dim, c0, nw, kb0, kbn),
                          out=b["wld"].t[:, 0:kbn, 0:nw])
            b["wld"].wait(p.vector)
            p.vector.reduce_sum(b["acc"][:, 1:2],
                                b["wld"].t.ap()[:, 0:kbn, 0:nw].rearrange(
                                    "p a b -> p (a b)"),
                                axis=AX.X, apply_absolute_value=True)
            p.vector.finc(s_v)
            b["wld"].consumed(s_v)
            p.vector.barrier(s_v)
            p.vector.tensor_add(b["acc"][:, 0:1], b["acc"][:, 0:1],
                                b["acc"][:, 1:2])
            p.vector.finc(s_v)
            p.vector.barrier(s_v)
    p.tensor.wait(s_v)
    p.tensor.wait(s_g)
    psg, gsl = ctx.gma_pool.acquire(p.tensor)
    p.tensor.matmul(psg[0:1, 0:1], b["acc"][:, 0:1], b["ones_col"][:, 0:1],
                    start=True, stop=True).inc(s_pe)
    p.vector.wait(s_pe)
    p.vector.tensor_scalar(b["cell"][0:1, 0:1], psg[0:1, 0:1],
                           1.0 / (k_dim * n_dim), None, ALU.mult)
    p.vector.finc(s_v)
    ctx.gma_pool.release(gsl, s_v)
    p.vector.barrier(s_v)
    p.vector.tensor_scalar(b["cell"][0:1, 0:1], b["cell"][0:1, 0:1],
                           clip_eps, None, ALU.max)
    p.vector.finc(s_v)
    p.tensor.wait(s_v)
    psg, gsl = ctx.gma_pool.acquire(p.tensor)
    p.tensor.matmul(psg[:, 0:1], ctx.ones1_row[:, 0:128], b["cell"][0:1, 0:1],
                    start=True, stop=True).inc(s_pe)
    p.vector.wait(s_pe)
    p.vector.tensor_copy(gcol[:, 0:1], psg[:, 0:1]).inc(s_v)
    ctx.gma_pool.release(gsl, s_v)
    p.vector.barrier(s_v)
    p.vector.reciprocal(gcol[:, 1:2], gcol[:, 0:1])
    p.vector.finc(s_v)
    p.vector.barrier(s_v)


def emit_wq_chunk(ctx, p, sems, wldbuf, wqdst, kbn, nw, rg_col):
    s_v = sems["v"]
    v = p.vector
    flat_in = wldbuf.ap()[:, 0:kbn, 0:nw].rearrange("p a b -> p (a b)")
    flat_out = wqdst.ap()[:, 0:kbn, 0:nw].rearrange("p a b -> p (a b)")
    N = kbn * nw
    v.tensor_scalar(flat_in, flat_in, rg_col, MAGIC, ALU.mult, ALU.add)
    v.finc(s_v)
    v.tensor_scalar(flat_in, flat_in, MAGIC, None, ALU.subtract)
    v.finc(s_v)
    v.tensor_scalar(flat_in, flat_in, 1.0, None, ALU.min)
    v.finc(s_v)
    v.tensor_scalar(flat_out, flat_in, -1.0, None, ALU.max)
    v.finc(s_v)


def build_l3f():
    nc, es, p, ctx = _common_ctx()
    ntile, ntok = NT8, NTOK
    H = 5632
    x = ctx.dram("x", [ntok, D], F32, kind="ExternalInput").ap()
    w1T = ctx.dram("w1T", [D, H], F32, kind="ExternalInput").ap()
    w3T = ctx.dram("w3T", [D, H], F32, kind="ExternalInput").ap()
    w2T = ctx.dram("w2T", [H, D], F32, kind="ExternalInput").ap()
    xnew = ctx.dram("xnew", [ntok, D], F32, kind="ExternalOutput").ap()
    kind = "ExternalOutput" if DEBUG_SCRATCH else "Internal"
    xq_scr = ctx.dram("xq_scr", [ntok, D], BF16, kind=kind).ap()
    h_scr = ctx.dram("h_scr", [ntok, H], F32, kind=kind).ap()
    hq_scr = ctx.dram("hq_scr", [ntok, H], BF16, kind=kind).ap()

    sems = _basic_sems(p)
    s_const, s_pe, s_v, s_s, s_g = (sems["const"], sems["pe"], sems["v"],
                                    sems["s"], sems["g"])
    ones1 = ctx.sb("ones1", [1, 128], F32)
    p.gpsimd.memset(ones1[:, :], 1.0).inc(s_g, 1)
    ctx.ones1_row = ones1
    ctx.gma_pool = PsumPool(ctx, "gmap", 1, (128, 512))
    wld = [DmaBuf(p, f"wld{k}", ctx.sb(f"wld{k}", [128, 16, 512], F32))
           for k in range(2)]
    wq = [ctx.sb(f"wq{k}", [128, 16, 512], BF16) for k in range(2)]
    ctx.gma_wld_t = wld[0].t
    g1 = ctx.sb("g1", [128, 2], F32)
    g2c = ctx.sb("g2c", [128, 2], F32)
    g3 = ctx.sb("g3", [128, 2], F32)
    amc_x = ctx.sb("amc_x", [128, NT8], F32)
    amc_h = ctx.sb("amc_h", [128, NT8], F32)
    gsx = ctx.sb("gsx", [128, 4], F32)

    emit_gamma(ctx, p, sems, w1T, D, H, g1, "g1")
    emit_gamma(ctx, p, sems, w3T, D, H, g3, "g3")
    emit_gamma(ctx, p, sems, w2T, H, D, g2c, "g2")

    bufs = dict(xt=DmaBuf(p, "lnx", ctx.sb("lnx", [128, D], F32)),
                xc=ctx.sb("lnc", [128, D], F32),
                stat=ctx.sb("lnstat", [128, 8], F32))
    outb = DmaBuf(p, "outb", ctx.sb("outb", [128, D], BF16))
    for i in range(ntile):
        def cb(i=i):
            outb.wait(p.vector)
            emit_q8_tile(ctx, p, sems, bufs["xc"], D, outb.t,
                         amc_x[:, i:i + 1])
            p.sync.wait(s_v)
            outb.store(p.sync, xq_scr[i * 128:(i + 1) * 128, :])
            outb.stored()
        emit_ln_f32_tile(ctx, p, sems, bufs, x, i, cb)
    outb.wait(p.sync)
    x_fm = ctx.sb("x_fm", [128, 16, ntok], BF16)
    xqld = DmaBuf(p, "xqld", None)
    for c in range(16):
        xqld.load(p.sync, xq_scr[:, c * 128:(c + 1) * 128],
                  out=x_fm[:, c, :], transpose=True)

    hg = ctx.sb("hg", [128, 512], F32)
    hv = DmaBuf(p, "hv", ctx.sb("hv", [128, 512], F32))
    pool_mm = PsumPool(ctx, "pm", 4)
    xqld.wait(p.tensor)
    p.tensor.wait(s_v)
    p.tensor.wait(s_g, 1)
    for c0 in range(0, H, 512):
        nw = min(512, H - c0)
        for j, wT in ((0, w1T), (1, w3T)):
            wld[j].load(p.sync, wT_chunk_view(wT, D, c0, nw),
                        out=wld[j].t[:, :, 0:nw])
            wld[j].wait(p.vector)
            p.vector.wait(s_pe)
            emit_wq_chunk(ctx, p, sems, wld[j].t, wq[j], 16, nw,
                          g1[:, 1:2] if j == 0 else g3[:, 1:2])
            wld[j].consumed(s_v)
        p.tensor.wait(s_v)
        for i in range(ntile):
            psG, slG = pool_mm.acquire(p.tensor)
            for c in range(16):
                p.tensor.matmul(psG[:, 0:nw], x_fm[:, c, i * 128:(i + 1) * 128],
                                wq[0][:, c, 0:nw], start=(c == 0), stop=(c == 15))
            psV, slV = pool_mm.acquire(p.tensor)
            for c in range(16):
                p.tensor.matmul(psV[:, 0:nw], x_fm[:, c, i * 128:(i + 1) * 128],
                                wq[1][:, c, 0:nw], start=(c == 0),
                                stop=(c == 15)).inc(s_pe)
            p.vector.wait(s_pe)
            p.vector.tensor_mul(gsx[:, 0:1], g1[:, 0:1], amc_x[:, i:i + 1])
            p.vector.tensor_mul(gsx[:, 1:2], g3[:, 0:1], amc_x[:, i:i + 1])
            p.vector.finc(s_v)
            p.vector.barrier(s_v)
            p.vector.tensor_scalar(gsx[:, 0:2], gsx[:, 0:2], 1.0 / 127, None,
                                   ALU.mult)
            p.vector.finc(s_v)
            p.vector.barrier(s_v)
            p.scalar.wait(s_v)
            p.scalar.activation(hg[:, 0:nw], psG[:, 0:nw], AF.Silu,
                                scale=gsx[:, 0:1])
            p.scalar.finc(s_s)
            pool_mm.release(slG, s_s)
            p.vector.wait(s_s)
            hv.wait(p.vector)
            p.vector.tensor_scalar(hv.t[:, 0:nw], psV[:, 0:nw], gsx[:, 1:2],
                                   None, ALU.mult)
            p.vector.finc(s_v)
            pool_mm.release(slV, s_v)
            p.vector.tensor_mul(hv.t[:, 0:nw], hv.t[:, 0:nw], hg[:, 0:nw])
            p.vector.finc(s_v)
            p.sync.wait(s_v)
            hv.store(p.sync, h_scr[i * 128:(i + 1) * 128, c0:c0 + nw],
                     src=hv.t[:, 0:nw])
            hv.stored()

    hld = DmaBuf(p, "hld",
                 wld[0].t.ap().rearrange("p a b -> p (a b)")[:, 0:H])
    hqb = DmaBuf(p, "hqb",
                 wq[0].ap().rearrange("p a b -> p (a b)")[:, 0:H])
    hv.wait(p.sync)
    p.sync.wait(s_pe)
    p.sync.wait(s_v)
    for i in range(ntile):
        hld.load(p.sync, h_scr[i * 128:(i + 1) * 128, :])
        hld.wait(p.vector)
        hqb.wait(p.vector)
        emit_q8_tile(ctx, p, sems, hld.t, H, hqb.t, amc_h[:, i:i + 1])
        hld.consumed(s_v)
        p.sync.wait(s_v)
        hqb.store(p.sync, hq_scr[i * 128:(i + 1) * 128, :])
        hqb.stored()

    epi_res, obufs = make_epi_residual(ctx, p, sems, x, xnew)
    ctx.l3f_xr = DmaBuf(p, "l3fxr", ctx.sb("l3fxr", [128, 512], F32))
    hq_fm = x_fm
    hqld = DmaBuf(p, "hqld", None)
    out_acc = ctx.sb("out_acc", [128, NT8, 512], F32)
    KC = H // 128
    hqb.wait(p.sync)
    for c0 in range(0, D, 512):
        nkb = 0
        for kb0 in range(0, KC, 16):
            kbn = min(16, KC - kb0)
            p.sync.wait(s_pe)
            for c in range(kbn):
                hqld.load(p.sync, hq_scr[:, (kb0 + c) * 128:(kb0 + c + 1) * 128],
                          out=hq_fm[:, c, :], transpose=True)
            hqb.wait(p.sync)
            wld[0].load(p.sync, wT_chunk_view(w2T, H, c0, 512, kb0, kbn),
                        out=wld[0].t[:, 0:kbn, :])
            wld[0].wait(p.vector)
            p.vector.wait(s_pe)
            emit_wq_chunk(ctx, p, sems, wld[0].t, wq[0], kbn, 512, g2c[:, 1:2])
            wld[0].consumed(s_v)
            p.tensor.wait(s_v)
            hqld.wait(p.tensor)
            for i in range(ntile):
                ps, sl = pool_mm.acquire(p.tensor)
                for c in range(kbn):
                    p.tensor.matmul(ps[:, 0:512],
                                    hq_fm[:, c, i * 128:(i + 1) * 128],
                                    wq[0][:, c, 0:512],
                                    start=(c == 0), stop=(c == kbn - 1))
                p.q["tensor"][-1].inc(s_pe)
                p.vector.wait(s_pe)
                if nkb == 0:
                    p.vector.tensor_copy(out_acc[:, i, :], ps[:, 0:512]
                                         ).inc(s_v)
                else:
                    p.vector.tensor_add(out_acc[:, i, :], out_acc[:, i, :],
                                        ps[:, 0:512]).inc(s_v)
                pool_mm.release(sl, s_v)
            nkb += 1
        for i in range(ntile):
            p.vector.tensor_mul(gsx[:, 2:3], g2c[:, 0:1], amc_h[:, i:i + 1])
            p.vector.finc(s_v)
            p.vector.barrier(s_v)
            p.vector.tensor_scalar(gsx[:, 3:4], gsx[:, 2:3], 1.0 / 127, None,
                                   ALU.mult)
            p.vector.finc(s_v)
            p.vector.barrier(s_v)
            p.vector.tensor_scalar(out_acc[:, i, :], out_acc[:, i, :],
                                   gsx[:, 3:4], None, ALU.mult)
            p.vector.finc(s_v)
            xr2 = obufs
        for i in range(ntile):
            kslot = i % 2
            xrb = ctx.l3f_xr
            xrb.load(p.sync, x[i * 128:(i + 1) * 128, c0:c0 + 512])
            xrb.wait(p.vector)
            if obufs[kslot].guard is not None:
                p.vector.wait(*obufs[kslot].guard)
                obufs[kslot].guard = None
            p.vector.tensor_add(obufs[kslot].t[:, 0:512], out_acc[:, i, :],
                                xrb.t[:, 0:512]).inc(s_v)
            xrb.consumed(s_v)
            p.sync.wait(s_v)
            obufs[kslot].store(p.sync,
                               xnew[i * 128:(i + 1) * 128, c0:c0 + 512])
            obufs[kslot].stored()
    for bq in obufs:
        bq.wait(p.sync)
    p.flush()
    es.close()
    return nc


def build_l2f():
    nc, es, p, ctx = _common_ctx()
    ntile, ntok = NT8, NTOK
    H = 8192
    x = ctx.dram("x", [ntok, D], F32, kind="ExternalInput").ap()
    gwT = ctx.dram("gwT", [D, H], F32, kind="ExternalInput").ap()
    vwT = ctx.dram("vwT", [D, H], F32, kind="ExternalInput").ap()
    owT = ctx.dram("owT", [H, D], F32, kind="ExternalInput").ap()
    gbr = ctx.dram("gbr", [1, H], F32, kind="ExternalInput").ap()
    vbr = ctx.dram("vbr", [1, H], F32, kind="ExternalInput").ap()
    obr = ctx.dram("obr", [1, D], F32, kind="ExternalInput").ap()
    identfd = ctx.dram("identf", [128, 128], F32, kind="ExternalInput").ap()
    xnew = ctx.dram("xnew", [ntok, D], F32, kind="ExternalOutput").ap()
    kind = "ExternalOutput" if DEBUG_SCRATCH else "Internal"
    xq_scr = ctx.dram("xq_scr", [ntok, D], BF16, kind=kind).ap()
    h_scr = ctx.dram("h_scr", [ntok, H], F32, kind=kind).ap()
    hq_scr = ctx.dram("hq_scr", [ntok, H], BF16, kind=kind).ap()
    gb_scl = ctx.dram("gb_scl", [1, H], F32, kind="Internal").ap()
    vb_scl = ctx.dram("vb_scl", [1, H], F32, kind="Internal").ap()

    sems = _basic_sems(p)
    s_const, s_pe, s_v, s_s, s_g = (sems["const"], sems["pe"], sems["v"],
                                    sems["s"], sems["g"])
    ones1 = ctx.sb("ones1", [1, 128], F32)
    p.gpsimd.memset(ones1[:, :], 1.0).inc(s_g, 1)
    ctx.ones1_row = ones1
    ctx.gma_pool = PsumPool(ctx, "gmap", 1, (128, 512))
    pool_tf = PsumPool(ctx, "ptf", 1, (128, 128), F32)
    identf = ctx.sb("identfsb", [128, 128], F32)
    p.sync.dma_start(out=identf[:, :], in_=identfd).inc(s_const, 16)
    wld = [DmaBuf(p, f"wld{k}", ctx.sb(f"wld{k}", [128, 16, 512], F32))
           for k in range(2)]
    wq = [ctx.sb(f"wq{k}", [128, 16, 512], BF16) for k in range(2)]
    ctx.gma_wld_t = wld[0].t
    gg = ctx.sb("gg", [128, 2], F32)
    gv = ctx.sb("gv", [128, 2], F32)
    go = ctx.sb("go", [128, 2], F32)
    amc_x = ctx.sb("amc_x", [128, NT8], F32)
    s_x = ctx.sb("s_x", [128, NT8], F32)
    amc_h = ctx.sb("amc_h", [128, NT8], F32)
    s_h = ctx.sb("s_h", [128, NT8], F32)
    gsx = ctx.sb("gsx", [128, 4], F32)
    sxrow = [ctx.sb(f"sxrow{i}", [1, 128], F32) for i in range(NT8)]
    shrow = [ctx.sb(f"shrow{i}", [1, 128], F32) for i in range(NT8)]
    brow = ctx.sb("brow", [1, 2048], F32)

    emit_gamma(ctx, p, sems, gwT, D, H, gg, "gg")
    emit_gamma(ctx, p, sems, vwT, D, H, gv, "gv")
    emit_gamma(ctx, p, sems, owT, H, D, go, "go")

    bldr = DmaBuf(p, "bldr", brow)
    for srcd, g_, dstd in ((gbr, gg, gb_scl), (vbr, gv, vb_scl)):
        for c0 in range(0, H, 2048):
            bldr.load(p.sync, srcd[:, c0:c0 + 2048])
            bldr.wait(p.vector)
            p.vector.tensor_scalar(brow[0:1, :], brow[0:1, :], g_[0:1, 1:2],
                                   None, ALU.mult)
            p.vector.finc(s_v)
            p.sync.wait(s_v)
            bldr.store(p.sync, dstd[:, c0:c0 + 2048])
            bldr.stored()

    bufs = dict(xt=DmaBuf(p, "lnx", ctx.sb("lnx", [128, D], F32)),
                xc=ctx.sb("lnc", [128, D], F32),
                stat=ctx.sb("lnstat", [128, 8], F32))
    outb = DmaBuf(p, "outb", ctx.sb("outb", [128, D], BF16))

    def mkrow(col_ap, row_t):
        p.tensor.wait(s_v)
        p.tensor.wait(s_const)
        pst, sl = pool_tf.acquire(p.tensor)
        p.tensor.transpose(pst[0:1, 0:128], col_ap, identf[:, :]).inc(s_pe)
        p.vector.wait(s_pe)
        p.vector.tensor_copy(row_t[0:1, :], pst[0:1, 0:128]).inc(s_v)
        pool_tf.release(sl, s_v)

    for i in range(ntile):
        def cb(i=i):
            outb.wait(p.vector)
            emit_q8_tile(ctx, p, sems, bufs["xc"], D, outb.t,
                         amc_x[:, i:i + 1])
            p.vector.tensor_copy(s_x[:, i:i + 1], ctx.sb_q8[:, 2:3]).inc(s_v)
            p.sync.wait(s_v)
            outb.store(p.sync, xq_scr[i * 128:(i + 1) * 128, :])
            outb.stored()
            mkrow(s_x[:, i:i + 1], sxrow[i])
        emit_ln_f32_tile(ctx, p, sems, bufs, x, i, cb)
    outb.wait(p.sync)
    x_fm = ctx.sb("x_fm", [128, 16, ntok], BF16)
    xqld = DmaBuf(p, "xqld", None)
    for c in range(16):
        xqld.load(p.sync, xq_scr[:, c * 128:(c + 1) * 128],
                  out=x_fm[:, c, :], transpose=True)

    hg = ctx.sb("hg", [128, 512], F32)
    hv = DmaBuf(p, "hv", ctx.sb("hv", [128, 512], F32))
    gbld = DmaBuf(p, "gbld", ctx.sb("gbld", [1, 512], F32))
    vbld = DmaBuf(p, "vbld", ctx.sb("vbld", [1, 512], F32))
    pool_mm = PsumPool(ctx, "pm", 4)
    xqld.wait(p.tensor)
    p.tensor.wait(s_v)
    p.tensor.wait(s_g, 1)
    for c0 in range(0, H, 512):
        for j, wT in ((0, gwT), (1, vwT)):
            wld[j].load(p.sync, wT_chunk_view(wT, D, c0, 512))
            wld[j].wait(p.vector)
            p.vector.wait(s_pe)
            emit_wq_chunk(ctx, p, sems, wld[j].t, wq[j], 16, 512,
                          gg[:, 1:2] if j == 0 else gv[:, 1:2])
            wld[j].consumed(s_v)
        gbld.load(p.sync, gb_scl[:, c0:c0 + 512])
        vbld.load(p.sync, vb_scl[:, c0:c0 + 512])
        gbld.wait(p.tensor)
        vbld.wait(p.tensor)
        p.tensor.wait(s_v)
        for i in range(ntile):
            psG, slG = pool_mm.acquire(p.tensor)
            for c in range(16):
                p.tensor.matmul(psG[:, 0:512], x_fm[:, c, i * 128:(i + 1) * 128],
                                wq[0][:, c, 0:512], start=(c == 0), stop=False)
            p.tensor.matmul(psG[:, 0:512], sxrow[i][0:1, :],
                            gbld.t[0:1, 0:512], start=False, stop=True)
            psV, slV = pool_mm.acquire(p.tensor)
            for c in range(16):
                p.tensor.matmul(psV[:, 0:512], x_fm[:, c, i * 128:(i + 1) * 128],
                                wq[1][:, c, 0:512], start=(c == 0), stop=False)
            p.tensor.matmul(psV[:, 0:512], sxrow[i][0:1, :],
                            vbld.t[0:1, 0:512], start=False,
                            stop=True).inc(s_pe)
            p.vector.wait(s_pe)
            p.vector.tensor_mul(gsx[:, 0:1], gg[:, 0:1], amc_x[:, i:i + 1])
            p.vector.tensor_mul(gsx[:, 1:2], gv[:, 0:1], amc_x[:, i:i + 1])
            p.vector.finc(s_v)
            p.vector.barrier(s_v)
            p.vector.tensor_scalar(gsx[:, 0:2], gsx[:, 0:2], 1.0 / 127, None,
                                   ALU.mult)
            p.vector.finc(s_v)
            p.vector.barrier(s_v)
            p.scalar.wait(s_v)
            p.scalar.activation(hg[:, 0:512], psG[:, 0:512], AF.Silu,
                                scale=gsx[:, 0:1])
            p.scalar.finc(s_s)
            pool_mm.release(slG, s_s)
            p.vector.wait(s_s)
            hv.wait(p.vector)
            p.vector.tensor_scalar(hv.t[:, 0:512], psV[:, 0:512], gsx[:, 1:2],
                                   None, ALU.mult)
            p.vector.finc(s_v)
            pool_mm.release(slV, s_v)
            p.vector.tensor_mul(hv.t[:, 0:512], hv.t[:, 0:512], hg[:, 0:512])
            p.vector.finc(s_v)
            p.sync.wait(s_v)
            hv.store(p.sync, h_scr[i * 128:(i + 1) * 128, c0:c0 + 512],
                     src=hv.t[:, 0:512])
            hv.stored()
        gbld.consumed(s_pe)
        vbld.consumed(s_pe)

    hld = DmaBuf(p, "hld",
                 wld[0].t.ap().rearrange("p a b -> p (a b)")[:, 0:H])
    hqb = DmaBuf(p, "hqb",
                 wq[0].ap().rearrange("p a b -> p (a b)")[:, 0:H])
    hv.wait(p.sync)
    p.sync.wait(s_pe)
    p.sync.wait(s_v)
    for i in range(ntile):
        hld.load(p.sync, h_scr[i * 128:(i + 1) * 128, :])
        hld.wait(p.vector)
        hqb.wait(p.vector)
        emit_q8_tile(ctx, p, sems, hld.t, H, hqb.t, amc_h[:, i:i + 1])
        p.vector.tensor_copy(s_h[:, i:i + 1], ctx.sb_q8[:, 2:3]).inc(s_v)
        hld.consumed(s_v)
        p.sync.wait(s_v)
        hqb.store(p.sync, hq_scr[i * 128:(i + 1) * 128, :])
        hqb.stored()
        mkrow(s_h[:, i:i + 1], shrow[i])

    obld = DmaBuf(p, "obld", ctx.sb("obld", [1, 2048], F32))
    obld.load(p.sync, obr)
    obld.wait(p.vector)
    p.vector.tensor_scalar(obld.t[0:1, :], obld.t[0:1, :], go[0:1, 1:2],
                           None, ALU.mult)
    p.vector.finc(s_v)

    epi_res, obufs = make_epi_residual(ctx, p, sems, x, xnew)
    l2fxr = DmaBuf(p, "l2fxr", ctx.sb("l2fxr", [128, 512], F32))
    hq_fm = x_fm
    hqld = DmaBuf(p, "hqld", None)
    out_acc = ctx.sb("out_acc", [128, NT8, 512], F32)
    KC = H // 128
    hqb.wait(p.sync)
    p.tensor.wait(s_v)
    for c0 in range(0, D, 512):
        nkb = 0
        for kb0 in range(0, KC, 16):
            kbn = min(16, KC - kb0)
            p.sync.wait(s_pe)
            for c in range(kbn):
                hqld.load(p.sync, hq_scr[:, (kb0 + c) * 128:(kb0 + c + 1) * 128],
                          out=hq_fm[:, c, :], transpose=True)
            hqb.wait(p.sync)
            wld[0].load(p.sync, wT_chunk_view(owT, H, c0, 512, kb0, kbn),
                        out=wld[0].t[:, 0:kbn, :])
            wld[0].wait(p.vector)
            p.vector.wait(s_pe)
            emit_wq_chunk(ctx, p, sems, wld[0].t, wq[0], kbn, 512, go[:, 1:2])
            wld[0].consumed(s_v)
            p.tensor.wait(s_v)
            hqld.wait(p.tensor)
            for i in range(ntile):
                ps, sl = pool_mm.acquire(p.tensor)
                for c in range(kbn):
                    p.tensor.matmul(ps[:, 0:512],
                                    hq_fm[:, c, i * 128:(i + 1) * 128],
                                    wq[0][:, c, 0:512],
                                    start=(c == 0), stop=False)
                if nkb == 0:
                    p.tensor.matmul(ps[:, 0:512], shrow[i][0:1, :],
                                    obld.t[0:1, c0:c0 + 512], start=False,
                                    stop=True)
                else:
                    p.q["tensor"][-1].kwargs["stop"] = True
                p.q["tensor"][-1].inc(s_pe)
                p.vector.wait(s_pe)
                if nkb == 0:
                    p.vector.tensor_copy(out_acc[:, i, :], ps[:, 0:512]
                                         ).inc(s_v)
                else:
                    p.vector.tensor_add(out_acc[:, i, :], out_acc[:, i, :],
                                        ps[:, 0:512]).inc(s_v)
                pool_mm.release(sl, s_v)
            nkb += 1
        for i in range(ntile):
            kslot = i % 2
            p.vector.tensor_mul(gsx[:, 2:3], go[:, 0:1], amc_h[:, i:i + 1])
            p.vector.finc(s_v)
            p.vector.barrier(s_v)
            p.vector.tensor_scalar(gsx[:, 3:4], gsx[:, 2:3], 1.0 / 127, None,
                                   ALU.mult)
            p.vector.finc(s_v)
            p.vector.barrier(s_v)
            p.vector.tensor_scalar(out_acc[:, i, :], out_acc[:, i, :],
                                   gsx[:, 3:4], None, ALU.mult)
            p.vector.finc(s_v)
            p.vector.barrier(s_v)
            l2fxr.load(p.sync, x[i * 128:(i + 1) * 128, c0:c0 + 512])
            l2fxr.wait(p.vector)
            if obufs[kslot].guard is not None:
                p.vector.wait(*obufs[kslot].guard)
                obufs[kslot].guard = None
            p.vector.tensor_add(obufs[kslot].t[:, 0:512], out_acc[:, i, :],
                                l2fxr.t[:, 0:512]).inc(s_v)
            l2fxr.consumed(s_v)
            p.sync.wait(s_v)
            obufs[kslot].store(p.sync,
                               xnew[i * 128:(i + 1) * 128, c0:c0 + 512])
            obufs[kslot].stored()
    for bq in obufs:
        bq.wait(p.sync)
    p.flush()
    es.close()
    return nc


def build_l3a1():
    nc, es, p, ctx = _common_ctx()
    ntile, ntok = NT8, NTOK
    x = ctx.dram("x", [ntok, D], F32, kind="ExternalInput").ap()
    wqT = ctx.dram("wqT", [D, D], F32, kind="ExternalInput").ap()
    wkT = ctx.dram("wkT", [D, D], F32, kind="ExternalInput").ap()
    wvT = ctx.dram("wvT", [D, D], F32, kind="ExternalInput").ap()
    bq = ctx.dram("bq", [1, D], F32, kind="ExternalInput").ap()
    bk = ctx.dram("bk", [1, D], F32, kind="ExternalInput").ap()
    bv = ctx.dram("bv", [1, D], F32, kind="ExternalInput").ap()
    q_scr = ctx.dram("q_scr", [ntok, D], BF16, kind="ExternalOutput").ap()
    k_scr = ctx.dram("k_scr", [ntok, D], BF16, kind="ExternalOutput").ap()
    v_scr = ctx.dram("v_scr", [ntok, D], BF16, kind="ExternalOutput").ap()
    xln_scr = ctx.dram("xln_scr", [ntok, D], BF16, kind="Internal").ap()

    sems = _basic_sems(p)
    s_const, s_g = sems["const"], sems["g"]
    ones1 = ctx.sb("ones1", [1, 128], F32)
    p.gpsimd.memset(ones1[:, :], 1.0).inc(s_g, 1)
    biases = {}
    for nm, d_ in (("bq", bq), ("bk", bk), ("bv", bv)):
        biases[nm] = ctx.sb(nm + "_sb", [1, D], F32)
        p.sync.dma_start(out=biases[nm][:, :], in_=d_).inc(s_const, 16)
    x_fm = ctx.sb("x_fm", [128, 16, ntok], BF16)
    xfmld = emit_ln_prelude(ctx, x, ntile, x_fm, xln_scr, sems)
    xfmld.wait(p.tensor)
    pool_mm = PsumPool(ctx, "pm", 3)
    wpool = WBufPool(ctx, "w_b", 16, 512, 2)
    SCALE = 1.0 / (128.0 ** 0.5)
    for nm, wT, bsb, dst, sc in (("q", wqT, biases["bq"], q_scr, SCALE),
                                 ("k", wkT, biases["bk"], k_scr, 1.0),
                                 ("v", wvT, biases["bv"], v_scr, 1.0)):
        epi, bufs = make_epi_store_bf16(ctx, p, sems, dst, "st" + nm, scale=sc)
        emit_proj(ctx, x_fm, 0, ntile, wT, D, D, sems,
                  {"w": wpool, "ps": pool_mm}, epi, bias_row=bsb, ones1=ones1)
        for b_ in bufs:
            b_.wait(p.sync)
    p.flush()
    es.close()
    return nc


def build_l3a2():
    nc, es, p, ctx = _common_ctx()
    ntile, ntok = NT8, NTOK
    x = ctx.dram("x", [ntok, D], F32, kind="ExternalInput").ap()
    q_scr = ctx.dram("q_scr", [ntok, D], BF16, kind="ExternalInput").ap()
    kf = ctx.dram("kf", [T, D], BF16, kind="ExternalInput").ap()
    vf = ctx.dram("vf", [T, D], BF16, kind="ExternalInput").ap()
    dmat = ctx.dram("dmat", [ntok, T], F32, kind="ExternalInput").ap()
    slopes = ctx.dram("slopes", [1, 16], F32, kind="ExternalInput").ap()
    woT = ctx.dram("woT", [D, D], F32, kind="ExternalInput").ap()
    bo = ctx.dram("bo", [1, D], F32, kind="ExternalInput").ap()
    identd = ctx.dram("ident", [128, 128], BF16, kind="ExternalInput").ap()
    xnew = ctx.dram("xnew", [ntok, D], F32, kind="ExternalOutput").ap()
    ao_scr = ctx.dram("ao_scr", [ntok, D], BF16, kind="Internal").ap()

    sems = _basic_sems(p)
    s_const, s_pe, s_v, s_s, s_g = (sems["const"], sems["pe"], sems["v"],
                                    sems["s"], sems["g"])
    ident = ctx.sb("identsb", [128, 128], BF16)
    ones1 = ctx.sb("ones1", [1, 128], F32)
    slop = ctx.sb("slop", [1, 16], F32)
    slopc = ctx.sb("slopc", [128, 16], F32)
    bo_sb = ctx.sb("bo_sb", [1, D], F32)
    p.sync.dma_start(out=ident[:, :], in_=identd).inc(s_const, 16)
    p.sync.dma_start(out=slop[:, :], in_=slopes).inc(s_const, 16)
    p.sync.dma_start(out=bo_sb[:, :], in_=bo).inc(s_const, 16)
    p.gpsimd.memset(ones1[:, :], 1.0).inc(s_g, 1)
    pool_s = PsumPool(ctx, "ps4", 1, (128, 2048))
    pool_t = PsumPool(ctx, "pt", 2, (128, 128), BF16)
    pool_o = PsumPool(ctx, "po", 1, (128, 128))
    p.tensor.wait(s_const)
    p.tensor.wait(s_g, 1)
    psb, sl = pool_o.acquire(p.tensor)
    p.tensor.matmul(psb[:, 0:16], ones1[:, 0:128], slop[0:1, 0:16],
                    start=True, stop=True).inc(s_pe)
    p.vector.wait(s_pe)
    p.vector.tensor_copy(slopc[:, :], psb[:, 0:16]).inc(s_v)
    pool_o.release(sl, s_v)
    p.vector.barrier(s_v)

    qh = DmaBuf(p, "qh", ctx.sb("qh", [128, ntok], BF16))
    kh = DmaBuf(p, "kh", ctx.sb("kh", [128, T], BF16))
    vh = DmaBuf(p, "vh", ctx.sb("vh", [128, 16, 128], BF16))
    dti = DmaBuf(p, "dti", ctx.sb("dti", [128, T], F32))
    s2 = ctx.sb("s2", [128, T], F32)
    probs = ctx.sb("probs", [128, T], BF16)
    pmT = ctx.sb("pmT", [128, 16, 128], BF16)
    aob = DmaBuf(p, "aob", ctx.sb("aob", [128, 128], BF16))
    stt_ = ctx.sb("sttbuf", [128, 4], F32)

    for h in range(16):
        qh.load(p.sync, q_scr[:, h * 128:(h + 1) * 128], transpose=True)
        kh.load(p.sync, kf[:, h * 128:(h + 1) * 128], transpose=True)
        for j in range(16):
            vh.load(p.sync, vf[j * 128:(j + 1) * 128, h * 128:(h + 1) * 128],
                    out=vh.t[:, j, :])
        qh.wait(p.tensor)
        kh.wait(p.tensor)
        vh.wait(p.tensor)
        for i in range(ntile):
            dti.load(p.sync, dmat[i * 128:(i + 1) * 128, :])
            psS, slS = pool_s.acquire(p.tensor)
            for c in range(4):
                p.tensor.matmul(psS[:, c * 512:(c + 1) * 512],
                                qh.t[:, i * 128:(i + 1) * 128],
                                kh.t[:, c * 512:(c + 1) * 512],
                                start=True, stop=True)
            p.q["tensor"][-1].inc(s_pe)
            dti.wait(p.vector)
            p.vector.wait(s_pe)
            p.vector.scalar_tensor_tensor(s2[:, :], dti.t[:, :],
                                          slopc[:, h:h + 1], psS[:, :],
                                          ALU.mult, ALU.add)
            p.vector.finc(s_v)
            dti.consumed(s_v)
            pool_s.release(slS, s_v)
            p.vector.barrier(s_v)
            p.vector.reduce_max(stt_[:, 0:1], s2[:, :], axis=AX.X)
            p.vector.barrier(s_v)
            p.vector.tensor_scalar(stt_[:, 1:2], stt_[:, 0:1], -1.0, None,
                                   ALU.mult)
            p.vector.finc(s_v)
            p.scalar.wait(s_v)
            p.scalar.activation(probs[:, :], s2[:, :], AF.Exp,
                                bias=stt_[:, 1:2], accum_out=stt_[:, 2:3])
            p.scalar.finc(s_s)
            p.tensor.wait(s_s)
            for c in range(16):
                pst, slT = pool_t.acquire(p.tensor)
                p.tensor.transpose(pst[:, :], probs[:, c * 128:(c + 1) * 128],
                                   ident[:, :]).inc(s_pe)
                p.vector.wait(s_pe)
                p.vector.tensor_copy(pmT[:, c, :], pst[:, :]).inc(s_v)
                pool_t.release(slT, s_v)
            p.vector.wait(s_s)
            p.vector.barrier(s_v)
            p.vector.reciprocal(stt_[:, 3:4], stt_[:, 2:3])
            p.vector.finc(s_v)
            p.vector.barrier(s_v)
            p.tensor.wait(s_v)
            psO, slO = pool_o.acquire(p.tensor)
            for c in range(16):
                p.tensor.matmul(psO[:, 0:128], pmT[:, c, :], vh.t[:, c, :],
                                start=(c == 0), stop=(c == 15))
            p.q["tensor"][-1].inc(s_pe)
            p.vector.wait(s_pe)
            aob.wait(p.vector)
            p.vector.tensor_scalar(aob.t[:, :], psO[:, 0:128], stt_[:, 3:4],
                                   None, ALU.mult).inc(s_v)
            pool_o.release(slO, s_v)
            p.sync.wait(s_v)
            aob.store(p.sync, ao_scr[i * 128:(i + 1) * 128,
                                     h * 128:(h + 1) * 128])
            aob.stored()
        qh.consumed(s_pe)
        kh.consumed(s_pe)
        vh.consumed(s_pe)

    aob.wait(p.sync)
    ao_fm = ctx.sb("ao_fm", [128, 16, ntok], BF16)
    aold = DmaBuf(p, "aold", None)
    for c in range(16):
        aold.load(p.sync, ao_scr[:, c * 128:(c + 1) * 128],
                  out=ao_fm[:, c, :], transpose=True)
    aold.wait(p.tensor)
    wpool = WBufPool(ctx, "w_b", 16, 512, 2)
    epi_res, obufs = make_epi_residual(ctx, p, sems, x, xnew)
    emit_proj(ctx, ao_fm, 0, ntile, woT, D, D, sems,
              {"w": wpool, "ps": pool_s}, epi_res,
              bias_row=bo_sb, ones1=ones1)
    for b_ in obufs:
        b_.wait(p.sync)
    p.flush()
    es.close()
    return nc


def emit_rope(ctx, p, sems, view3, cosv, sinv, tmp):
    s_v = sems["v"]
    v = p.vector
    xe = view3[:, :, 0:64:2]
    xo = view3[:, :, 1:64:2]
    nh = view3.shape[1]
    c3 = cosv.unsqueeze(1).to_broadcast([128, nh, 32])
    s3 = sinv.unsqueeze(1).to_broadcast([128, nh, 32])
    n = nh * 32
    t_re = tmp[:, 0:n].rearrange("p (h k) -> p h k", k=32)
    t_ro = tmp[:, n:2 * n].rearrange("p (h k) -> p h k", k=32)
    v.tensor_tensor(t_re, xe, c3, ALU.mult)
    v.tensor_tensor(t_ro, xo, c3, ALU.mult)
    v.finc(s_v)
    v.barrier(s_v)
    v.tensor_tensor(tmp[:, 2 * n:3 * n].rearrange("p (h k) -> p h k", k=32),
                    xo, s3, ALU.mult)
    v.tensor_tensor(tmp[:, 3 * n:4 * n].rearrange("p (h k) -> p h k", k=32),
                    xe, s3, ALU.mult)
    v.finc(s_v)
    v.barrier(s_v)
    v.tensor_sub(xe, t_re, tmp[:, 2 * n:3 * n].rearrange("p (h k) -> p h k",
                                                         k=32))
    v.tensor_add(xo, t_ro, tmp[:, 3 * n:4 * n].rearrange("p (h k) -> p h k",
                                                         k=32))
    v.finc(s_v)
    v.barrier(s_v)


def build_l1a1():
    nc, es, p, ctx = _common_ctx()
    ntile, ntok = NT8, NTOK
    QL, KVL, NKV = 1536, 512, 3072
    x = ctx.dram("x", [ntok, D], F32, kind="ExternalInput").ap()
    wqa = ctx.dram("wqa", [D, QL], F32, kind="ExternalInput").ap()
    wqb = ctx.dram("wqb", [QL, D], F32, kind="ExternalInput").ap()
    wkva = ctx.dram("wkva", [D, 576], F32, kind="ExternalInput").ap()
    wkvb = ctx.dram("wkvb", [KVL, NKV], F32, kind="ExternalInput").ap()
    cosd = ctx.dram("cosd", [ntok, 32], F32, kind="ExternalInput").ap()
    sind = ctx.dram("sind", [ntok, 32], F32, kind="ExternalInput").ap()
    q_scr = ctx.dram("q_scr", [ntok, D], BF16, kind="ExternalOutput").ap()
    kpe_scr = ctx.dram("kpe_scr", [ntok, 64], BF16, kind="ExternalOutput").ap()
    kvout_scr = ctx.dram("kvout_scr", [ntok, NKV], BF16,
                         kind="ExternalOutput").ap()
    ql_scr = ctx.dram("ql_scr", [ntok, QL], BF16, kind="Internal").ap()
    kv_scr = ctx.dram("kv_scr", [ntok, 576], BF16, kind="Internal").ap()
    xln_scr = ctx.dram("xln_scr", [ntok, D], BF16, kind="Internal").ap()

    sems = _basic_sems(p)
    s_const, s_pe, s_v, s_s, s_g = (sems["const"], sems["pe"], sems["v"],
                                    sems["s"], sems["g"])
    cos_sb = ctx.sb("cos_sb", [128, NT8, 32], F32)
    sin_sb = ctx.sb("sin_sb", [128, NT8, 32], F32)
    csld = DmaBuf(p, "csld", None)
    for i in range(ntile):
        csld.load(p.sync, cosd[i * 128:(i + 1) * 128, :], out=cos_sb[:, i, :])
        csld.load(p.sync, sind[i * 128:(i + 1) * 128, :], out=sin_sb[:, i, :])

    x_fm = ctx.sb("x_fm", [128, 16, ntok], BF16)
    xfmld = emit_ln_prelude(ctx, x, ntile, x_fm, xln_scr, sems)
    xfmld.wait(p.tensor)
    pool_mm = PsumPool(ctx, "pm", 3)
    wpool = WBufPool(ctx, "w_b", 16, 512, 2)
    epi_ql, b1 = make_epi_store_bf16(ctx, p, sems, ql_scr, "ql")
    emit_proj(ctx, x_fm, 0, ntile, wqa, D, QL, sems,
              {"w": wpool, "ps": pool_mm}, epi_ql)
    epi_kv, b2 = make_epi_store_bf16(ctx, p, sems, kv_scr, "kvc")
    emit_proj(ctx, x_fm, 0, ntile, wkva, D, 576, sems,
              {"w": wpool, "ps": pool_mm}, epi_kv)
    for b_ in b1 + b2:
        b_.wait(p.sync)
    ql_fm = ctx.sb("ql_fm", [128, 12, ntok], BF16)
    qlld = DmaBuf(p, "qlld", None)
    for c in range(12):
        qlld.load(p.sync, ql_scr[:, c * 128:(c + 1) * 128],
                  out=ql_fm[:, c, :], transpose=True)
    qlld.wait(p.tensor)
    ropetmp = ctx.sb("ropetmp", [128, 512], F32)
    qf = ctx.sb("qf", [128, 512], F32)
    qsb = [DmaBuf(p, f"qsb{k}", ctx.sb(f"qsb{k}", [128, 512], BF16))
           for k in range(2)]

    def epi_q(i, c0, nw, ps, slot, ppool):
        kslot = i % 2
        p.vector.wait(sems["pe"])
        p.vector.wait(s_s)
        p.vector.tensor_copy(qf[:, 0:nw], ps[:, 0:nw]).inc(s_v)
        ppool.release(slot, s_v)
        p.vector.barrier(s_v)
        qv = qf.ap()[:, 0:512].rearrange("p (h d) -> p h d", d=128)[:, :, 64:128]
        emit_rope(ctx, p, sems, qv, cos_sb[:, i, :], sin_sb[:, i, :], ropetmp)
        if qsb[kslot].guard is not None:
            p.vector.wait(*qsb[kslot].guard)
            qsb[kslot].guard = None
        p.vector.tensor_copy(qsb[kslot].t[:, 0:nw], qf[:, 0:nw]).inc(s_v)
        p.sync.wait(s_v)
        qsb[kslot].store(p.sync, q_scr[i * 128:(i + 1) * 128, c0:c0 + nw])
        qsb[kslot].stored()

    emit_proj(ctx, ql_fm, 0, ntile, wqb, QL, D, sems,
              {"w": wpool, "ps": pool_mm}, epi_q)

    kpeb = DmaBuf(p, "kpeb", ctx.sb("kpeb", [128, 64], BF16))
    kpef = ctx.sb("kpef", [128, 64], F32)
    for i in range(ntile):
        kpeb.load(p.sync, kv_scr[i * 128:(i + 1) * 128, 512:576])
        kpeb.wait(p.vector)
        p.vector.tensor_copy(kpef[:, :], kpeb.t[:, :]).inc(s_v)
        p.vector.barrier(s_v)
        emit_rope(ctx, p, sems,
                  kpef.ap().unsqueeze(1), cos_sb[:, i, :], sin_sb[:, i, :],
                  ropetmp)
        p.vector.tensor_copy(kpeb.t[:, :], kpef[:, :]).inc(s_v)
        p.sync.wait(s_v)
        kpeb.store(p.sync, kpe_scr[i * 128:(i + 1) * 128, :])
        kpeb.stored()

    ckv_fm = ctx.sb("ckv_fm", [128, 4, ntok], BF16)
    ckld = DmaBuf(p, "ckld", None)
    for c in range(4):
        ckld.load(p.sync, kv_scr[:, c * 128:(c + 1) * 128],
                  out=ckv_fm[:, c, :], transpose=True)
    ckld.wait(p.tensor)
    epi_ko, b3 = make_epi_store_bf16(ctx, p, sems, kvout_scr, "kvo")
    emit_proj(ctx, ckv_fm, 0, ntile, wkvb, KVL, NKV, sems,
              {"w": wpool, "ps": pool_mm}, epi_ko)
    for b_ in b3:
        b_.wait(p.sync)
    p.flush()
    es.close()
    return nc


def build_l1a2():
    nc, es, p, ctx = _common_ctx()
    ntile, ntok = NT8, NTOK
    NKV = 3072
    x = ctx.dram("x", [ntok, D], F32, kind="ExternalInput").ap()
    q_scr = ctx.dram("q_scr", [ntok, D], BF16, kind="ExternalInput").ap()
    kvf = ctx.dram("kvf", [T, NKV], BF16, kind="ExternalInput").ap()
    kpef = ctx.dram("kpef", [T, 128], BF16, kind="ExternalInput").ap()
    maskd = ctx.dram("maskd", [ntok, T], BF16, kind="ExternalInput").ap()
    wo = ctx.dram("wo", [D, D], F32, kind="ExternalInput").ap()
    identd = ctx.dram("ident", [128, 128], BF16, kind="ExternalInput").ap()
    xnew = ctx.dram("xnew", [ntok, D], F32, kind="ExternalOutput").ap()
    ao_scr = ctx.dram("ao_scr", [ntok, D], BF16, kind="Internal").ap()

    sems = _basic_sems(p)
    s_const, s_pe, s_v, s_s, s_g = (sems["const"], sems["pe"], sems["v"],
                                    sems["s"], sems["g"])
    ident = ctx.sb("identsb", [128, 128], BF16)
    p.sync.dma_start(out=ident[:, :], in_=identd).inc(s_const, 16)
    mask_sb = ctx.sb("mask_sb", [128, NT8, T], BF16)
    mld = DmaBuf(p, "mld", None)
    for i in range(ntile):
        mld.load(p.sync, maskd[i * 128:(i + 1) * 128, :], out=mask_sb[:, i, :])

    pool_s = PsumPool(ctx, "ps4", 1, (128, 2048))
    pool_t = PsumPool(ctx, "pt", 2, (128, 128), BF16)
    pool_o = PsumPool(ctx, "po", 1, (128, 128))
    qh = DmaBuf(p, "qh", ctx.sb("qh", [128, ntok], BF16))
    kh = DmaBuf(p, "kh", ctx.sb("kh", [128, T], BF16))
    kpe_part = DmaBuf(p, "kpe_part", None)
    vh = DmaBuf(p, "vh", ctx.sb("vh", [128, 16, 128], BF16))
    probs = ctx.sb("probs", [128, T], BF16)
    pm = ctx.sb("pmbuf", [128, T], BF16)
    pmT = ctx.sb("pmT", [128, 16, 128], BF16)
    aob = DmaBuf(p, "aob", ctx.sb("aob", [128, 128], BF16))
    stt_ = ctx.sb("sttbuf", [128, 4], F32)
    SCALE = 1.0 / (128.0 ** 0.5)

    kpeT = ctx.sb("kpeT", [128, T], BF16)
    kpe_part.load(p.sync, kpef, out=kpeT[:, 0:T], transpose=True)
    kpe_part.wait(p.vector)
    mld.wait(p.tensor)
    p.tensor.wait(s_const)
    for h in range(16):
        qh.load(p.sync, q_scr[:, h * 128:(h + 1) * 128], transpose=True)
        kh.load(p.sync, kvf[:, h * 192:h * 192 + 128],
                out=kh.t[0:128, 0:T], transpose=True)
        kh.wait(p.vector)
        p.vector.wait(s_pe)
        p.vector.tensor_copy(kh.t[64:128, 0:T], kpeT[0:64, 0:T]).inc(s_v)
        p.tensor.wait(s_v)
        for j in range(16):
            vh.load(p.sync, kvf[j * 128:(j + 1) * 128,
                                h * 192 + 64:h * 192 + 192],
                    out=vh.t[:, j, :])
        qh.wait(p.tensor)
        kh.wait(p.tensor)
        vh.wait(p.tensor)
        for i in range(ntile):
            psS, slS = pool_s.acquire(p.tensor)
            for c in range(4):
                p.tensor.matmul(psS[:, c * 512:(c + 1) * 512],
                                qh.t[:, i * 128:(i + 1) * 128],
                                kh.t[:, c * 512:(c + 1) * 512],
                                start=True, stop=True)
            p.q["tensor"][-1].inc(s_pe)
            p.vector.wait(s_pe)
            p.vector.reduce_max(stt_[:, 0:1], psS[:, :], axis=AX.X)
            p.vector.barrier(s_v)
            p.vector.tensor_scalar(stt_[:, 1:2], stt_[:, 0:1], -SCALE, None,
                                   ALU.mult)
            p.vector.finc(s_v)
            p.scalar.wait(s_v)
            p.scalar.activation(probs[:, :], psS[:, :], AF.Exp,
                                bias=stt_[:, 1:2], scale=SCALE)
            p.scalar.finc(s_s)
            pool_s.release(slS, s_s)
            p.vector.wait(s_s)
            p.vector.scalar_tensor_tensor(pm[:, :], probs[:, :], 1.0,
                                          mask_sb[:, i, :], ALU.mult, ALU.mult,
                                          accum_out=stt_[:, 2:3])
            p.vector.finc(s_v)
            p.tensor.wait(s_v)
            for c in range(16):
                pst, slT = pool_t.acquire(p.tensor)
                p.tensor.transpose(pst[:, :], pm[:, c * 128:(c + 1) * 128],
                                   ident[:, :]).inc(s_pe)
                p.vector.wait(s_pe)
                p.vector.tensor_copy(pmT[:, c, :], pst[:, :]).inc(s_v)
                pool_t.release(slT, s_v)
            p.vector.barrier(s_v)
            p.vector.reciprocal(stt_[:, 3:4], stt_[:, 2:3])
            p.vector.finc(s_v)
            p.vector.barrier(s_v)
            p.tensor.wait(s_v)
            psO, slO = pool_o.acquire(p.tensor)
            for c in range(16):
                p.tensor.matmul(psO[:, 0:128], pmT[:, c, :], vh.t[:, c, :],
                                start=(c == 0), stop=(c == 15))
            p.q["tensor"][-1].inc(s_pe)
            p.vector.wait(s_pe)
            aob.wait(p.vector)
            p.vector.tensor_scalar(aob.t[:, :], psO[:, 0:128], stt_[:, 3:4],
                                   None, ALU.mult).inc(s_v)
            pool_o.release(slO, s_v)
            p.sync.wait(s_v)
            aob.store(p.sync, ao_scr[i * 128:(i + 1) * 128,
                                     h * 128:(h + 1) * 128])
            aob.stored()
        qh.consumed(s_pe)
        kh.consumed(s_pe)
        vh.consumed(s_pe)

    aob.wait(p.sync)
    ao_fm = ctx.sb("ao_fm", [128, 16, ntok], BF16)
    aold = DmaBuf(p, "aold", None)
    for c in range(16):
        aold.load(p.sync, ao_scr[:, c * 128:(c + 1) * 128],
                  out=ao_fm[:, c, :], transpose=True)
    aold.wait(p.tensor)
    wpool = WBufPool(ctx, "w_b", 16, 512, 2)
    epi_res, obufs = make_epi_residual(ctx, p, sems, x, xnew)
    emit_proj(ctx, ao_fm, 0, ntile, wo, D, D, sems,
              {"w": wpool, "ps": pool_s}, epi_res)
    for b_ in obufs:
        b_.wait(p.sync)
    p.flush()
    es.close()
    return nc


def build_l2a():
    nc, es, p, ctx = _common_ctx()
    ntile, ntok = 16, T
    DH = 128
    x = ctx.dram("x", [ntok, D], F32, kind="ExternalInput").ap()
    wqT = ctx.dram("wqT", [D, DH], F32, kind="ExternalInput").ap()
    wkT = ctx.dram("wkT", [D, DH], F32, kind="ExternalInput").ap()
    wvT = ctx.dram("wvT", [D, DH], F32, kind="ExternalInput").ap()
    woT = ctx.dram("woT", [DH, D], F32, kind="ExternalInput").ap()
    m_in = ctx.dram("m_in", [DH, DH], F32, kind="ExternalInput").ap()
    z_in = ctx.dram("z_in", [DH, 1], F32, kind="ExternalInput").ap()
    gate = ctx.dram("gate", [1, 1], F32, kind="ExternalInput").ap()
    trild = ctx.dram("trild", [128, 128], BF16, kind="ExternalInput").ap()
    identd = ctx.dram("ident", [128, 128], BF16, kind="ExternalInput").ap()
    xnew = ctx.dram("xnew", [ntok, D], F32, kind="ExternalOutput").ap()
    xln_scr = ctx.dram("xln_scr", [ntok, D], BF16, kind="Internal").ap()

    sems = _basic_sems(p)
    s_const, s_pe, s_v, s_s, s_g = (sems["const"], sems["pe"], sems["v"],
                                    sems["s"], sems["g"])
    EPS = 1e-6
    ident = ctx.sb("identsb", [128, 128], BF16)
    tril = ctx.sb("trilsb", [128, 128], BF16)
    ones1 = ctx.sb("ones1", [1, 128], F32)
    gcell = ctx.sb("gcell", [1, 2], F32)
    gcol = ctx.sb("gcol", [128, 2], F32)
    m_aug = ctx.sb("m_aug", [128, 132], BF16)
    s_aug = ctx.sb("s_aug", [128, 132], BF16)
    p.sync.dma_start(out=ident[:, :], in_=identd).inc(s_const, 16)
    p.sync.dma_start(out=tril[:, :], in_=trild).inc(s_const, 16)
    p.gpsimd.dma_start(out=m_aug[:, 0:128], in_=m_in).inc(s_const, 16)
    p.gpsimd.dma_start(out=m_aug[:, 128:129], in_=z_in).inc(s_const, 16)
    p.sync.dma_start(out=gcell[0:1, 0:1], in_=gate).inc(s_const, 16)
    p.gpsimd.memset(ones1[:, :], 1.0)
    p.gpsimd.memset(m_aug[:, 129:132], 0.0).inc(s_g, 1)
    p.scalar.wait(s_const)
    p.scalar.activation(gcell[0:1, 1:2], gcell[0:1, 0:1], AF.Sigmoid)
    p.scalar.finc(s_s)
    pool_o = PsumPool(ctx, "po", 2, (128, 512))
    p.tensor.wait(s_s)
    p.tensor.wait(s_g, 1)
    psb, sl = pool_o.acquire(p.tensor)
    p.tensor.matmul(psb[:, 0:1], ones1[:, 0:128], gcell[0:1, 1:2],
                    start=True, stop=True).inc(s_pe)
    p.vector.wait(s_pe)
    p.vector.tensor_copy(gcol[:, 0:1], psb[:, 0:1]).inc(s_v)
    pool_o.release(sl, s_v)
    p.vector.barrier(s_v)
    p.vector.tensor_scalar(gcol[:, 1:2], gcol[:, 0:1], -1.0, 1.0,
                           ALU.mult, ALU.add)
    p.vector.finc(s_v)
    p.vector.barrier(s_v)

    x_fm = ctx.sb("x_fm", [128, 16, ntok], BF16)
    xfmld = emit_ln_prelude(ctx, x, ntile, x_fm, xln_scr, sems)
    xfmld.wait(p.tensor)

    pq_t = ctx.sb("pq_t", [128, 16, 128], BF16)
    pk_t = ctx.sb("pk_t", [128, 16, 128], BF16)
    pqT = ctx.sb("pqT", [128, 16, 128], BF16)
    pkT = ctx.sb("pkT", [128, 16, 128], BF16)
    v_aug = ctx.sb("v_aug", [128, 16, 132], BF16)
    sigt = ctx.sb("sigt", [128, 256], F32)
    pool_t = PsumPool(ctx, "pt", 2, (128, 128), BF16)
    pool_mm = PsumPool(ctx, "pm", 2, (128, 132))
    p.gpsimd.wait(s_g, 1)
    p.gpsimd.memset(v_aug[:, :, 128:129], 1.0)
    p.gpsimd.memset(v_aug[:, :, 129:132], 0.0).inc(s_g)
    wpool = WBufPool(ctx, "w_b", 16, 512, 1)

    def mk_sigma(dst_tm, dstT):
        def epi(i, c0, nw, ps, slot, ppool):
            p.vector.wait(s_pe)
            p.vector.wait(s_s)
            p.vector.tensor_scalar(sigt[:, 0:128], ps[:, 0:128], 0.0, None,
                                   ALU.min)
            p.vector.tensor_scalar(sigt[:, 128:256], ps[:, 0:128], 0.0, None,
                                   ALU.max).inc(s_v)
            ppool.release(slot, s_v)
            p.scalar.wait(s_v)
            p.scalar.activation(sigt[:, 0:128], sigt[:, 0:128], AF.Exp)
            p.scalar.finc(s_s)
            p.vector.wait(s_s)
            p.vector.tensor_add(dst_tm[:, i, :], sigt[:, 0:128],
                                sigt[:, 128:256])
            p.vector.finc(s_v)
            p.tensor.wait(s_v)
            pst, slT = pool_t.acquire(p.tensor)
            p.tensor.transpose(pst[:, :], dst_tm[:, i, :], ident[:, :]
                               ).inc(s_pe)
            p.vector.wait(s_pe)
            p.vector.tensor_copy(dstT[:, i, :], pst[:, :]).inc(s_v)
            pool_t.release(slT, s_v)
        return epi

    emit_proj(ctx, x_fm, 0, ntile, wqT, D, DH, sems,
              {"w": wpool, "ps": pool_o}, mk_sigma(pq_t, pqT))
    emit_proj(ctx, x_fm, 0, ntile, wkT, D, DH, sems,
              {"w": wpool, "ps": pool_o}, mk_sigma(pk_t, pkT))

    def epi_v(i, c0, nw, ps, slot, ppool):
        p.scalar.wait(s_pe)
        p.scalar.wait(s_g)
        p.scalar.copy(v_aug[:, i, 0:128], ps[:, 0:128]).inc(s_s)
        ppool.release(slot, s_s)

    emit_proj(ctx, x_fm, 0, ntile, wvT, D, DH, sems,
              {"w": wpool, "ps": pool_o}, epi_v)

    woTb = DmaBuf(p, "woTb", ctx.sb("woTb", [128, D], BF16))
    woTb.load(p.gpsimd, woT)

    ps_sacc = ctx.ps("ps_sacc", [128, 132], F32)
    smb = ctx.sb("smb", [128, 128], BF16)
    smTb = ctx.sb("smTb", [128, 128], BF16)
    nloc = ctx.sb("nloc", [128, 132], F32)
    nmem = ctx.sb("nmem", [128, 132], F32)
    dvec = ctx.sb("dvec", [128, 4], F32)
    aot = ctx.sb("aot", [128, 128], BF16)
    aoT = ctx.sb("aoT", [128, 128], BF16)
    epi_res, obufs = make_epi_residual(ctx, p, sems, x, xnew)
    p.tensor.wait(s_v)
    p.tensor.wait(s_s)
    for i in range(ntile):
        psQK, slQK = pool_mm.acquire(p.tensor)
        p.tensor.matmul(psQK[:, 0:128], pqT[:, i, :], pkT[:, i, :],
                        start=True, stop=True).inc(s_pe)
        p.vector.wait(s_pe)
        p.vector.tensor_mul(smb[:, :], psQK[:, 0:128], tril[:, :]).inc(s_v)
        pool_mm.release(slQK, s_v)
        p.tensor.wait(s_v)
        pst, slT = pool_t.acquire(p.tensor)
        p.tensor.transpose(pst[:, :], smb[:, :], ident[:, :]).inc(s_pe)
        p.vector.wait(s_pe)
        p.vector.tensor_copy(smTb[:, :], pst[:, :]).inc(s_v)
        pool_t.release(slT, s_v)
        p.tensor.wait(s_v)
        psN, slN = pool_mm.acquire(p.tensor)
        p.tensor.matmul(psN[:, 0:132], smTb[:, :], v_aug[:, i, 0:132],
                        start=True, stop=(i == 0))
        if i > 0:
            p.tensor.matmul(psN[:, 0:132], pqT[:, i, :], s_aug[:, 0:132],
                            start=False, stop=True)
        psM, slM = pool_mm.acquire(p.tensor)
        p.tensor.matmul(psM[:, 0:132], pqT[:, i, :], m_aug[:, 0:132],
                        start=True, stop=True).inc(s_pe)
        p.tensor.matmul(ps_sacc[:, 0:132], pk_t[:, i, :], v_aug[:, i, 0:132],
                        start=(i == 0), stop=(i == ntile - 1)).inc(s_pe)
        p.vector.wait(s_pe)
        p.vector.tensor_copy(s_aug[:, 0:132], ps_sacc[:, 0:132]).inc(s_v)
        p.vector.tensor_copy(nloc[:, :], psN[:, 0:132])
        p.vector.tensor_copy(nmem[:, :], psM[:, 0:132]).inc(s_v)
        pool_mm.release(slN, s_v)
        pool_mm.release(slM, s_v)
        p.vector.barrier(s_v)
        p.vector.tensor_scalar(dvec[:, 0:1], nloc[:, 128:129], EPS, None,
                               ALU.add)
        p.vector.tensor_scalar(dvec[:, 1:2], nmem[:, 128:129], EPS, None,
                               ALU.add).inc(s_v)
        p.vector.barrier(s_v)
        p.vector.reciprocal(dvec[:, 2:3], dvec[:, 0:1])
        p.vector.finc(s_v)
        p.vector.barrier(s_v)
        p.vector.reciprocal(dvec[:, 3:4], dvec[:, 1:2])
        p.vector.finc(s_v)
        p.vector.barrier(s_v)
        p.vector.tensor_scalar(nloc[:, 0:128], nloc[:, 0:128], dvec[:, 2:3],
                               None, ALU.mult)
        p.vector.tensor_scalar(nmem[:, 0:128], nmem[:, 0:128], dvec[:, 3:4],
                               None, ALU.mult).inc(s_v)
        p.vector.barrier(s_v)
        p.vector.tensor_scalar(nloc[:, 0:128], nloc[:, 0:128], gcol[:, 1:2],
                               None, ALU.mult)
        p.vector.tensor_scalar(nmem[:, 0:128], nmem[:, 0:128], gcol[:, 0:1],
                               None, ALU.mult).inc(s_v)
        p.vector.barrier(s_v)
        p.vector.tensor_add(aot[:, :], nloc[:, 0:128], nmem[:, 0:128])
        p.vector.finc(s_v)
        p.tensor.wait(s_v)
        pst, slT = pool_t.acquire(p.tensor)
        p.tensor.transpose(pst[:, :], aot[:, :], ident[:, :]).inc(s_pe)
        p.vector.wait(s_pe)
        p.vector.tensor_copy(aoT[:, :], pst[:, :]).inc(s_v)
        pool_t.release(slT, s_v)
        woTb.wait(p.tensor)
        p.tensor.wait(s_v)
        for c0 in range(0, D, 512):
            psO, slO = pool_o.acquire(p.tensor)
            p.tensor.matmul(psO[:, 0:512], aoT[:, :], woTb.t[:, c0:c0 + 512],
                            start=True, stop=True).inc(s_pe)
            epi_res(i, c0, 512, psO, slO, pool_o)
    for b_ in obufs:
        b_.wait(p.sync)
    p.flush()
    es.close()
    return nc


def _run8(nc, in_maps):
    return run_bass_kernel_spmd(nc, in_maps, core_ids=list(range(NCORES)))


def _gather_halves(res, key):
    out = np.empty((B, T, D), np.float32)
    for c in range(NCORES):
        b, rows, half = _core_slices(c)
        out[b, rows] = np.asarray(res.results[c][key], np.float32)
    return out


def run_l0f(x_full, w):
    nc = cached_build("l0f", build_l0f)
    w_raw = as_np(w["L0_ffn_weight"])
    bias = as_np(w["L0_ffn_bias"]).reshape(1, D)
    in_maps = []
    for c in range(NCORES):
        b, rows, half = _core_slices(c)
        in_maps.append(dict(x=as_np(x_full[b, rows]), w_raw=w_raw, bias=bias))
    return _gather_halves(_run8(nc, in_maps), "xnew")


def run_l1a(x_full, w):
    nc1 = cached_build("l1a1", build_l1a1)
    Wqa = as_np(w["L1_attn_Wqa"])
    Wqb = as_np(w["L1_attn_Wqb"])
    Wkva = as_np(w["L1_attn_Wkva"])
    Wkvb = as_np(w["L1_attn_Wkvb"])
    Wo = as_np(w["L1_attn_Wo"])
    half_ = 32
    inv = 1.0 / 10000.0 ** (np.arange(half_) * 2.0 / 64)
    fr = np.arange(T)[:, None] * inv[None, :]
    cosT = np.cos(fr).astype(np.float32)
    sinT = np.sin(fr).astype(np.float32)
    in_maps = []
    for c in range(NCORES):
        b, rows, half = _core_slices(c)
        in_maps.append(dict(x=as_np(x_full[b, rows]), wqa=Wqa, wqb=Wqb,
                            wkva=Wkva, wkvb=Wkvb,
                            cosd=as_np(cosT[rows]), sind=as_np(sinT[rows])))
    res1 = _run8(nc1, in_maps)
    nc2 = cached_build("l1a2", build_l1a2)
    in_maps2 = []
    masks = []
    for half in range(2):
        m = np.zeros((NTOK, T), np.float32)
        for t in range(NTOK):
            m[t, : half * NTOK + t + 1] = 1.0
        masks.append(np.ascontiguousarray(m.astype(BF)))
    for c in range(NCORES):
        b, rows, half = _core_slices(c)
        kvfull = np.concatenate(
            [np.asarray(res1.results[2 * b]["kvout_scr"]),
             np.asarray(res1.results[2 * b + 1]["kvout_scr"])], 0)
        kpefull = np.concatenate(
            [np.asarray(res1.results[2 * b]["kpe_scr"]),
             np.asarray(res1.results[2 * b + 1]["kpe_scr"])], 0)
        kpepad = np.zeros((T, 128), BF)
        kpepad[:, 0:64] = kpefull
        in_maps2.append(dict(x=as_np(x_full[b, rows]),
                             q_scr=np.ascontiguousarray(res1.results[c]["q_scr"]),
                             kvf=np.ascontiguousarray(kvfull),
                             kpef=kpepad, maskd=masks[half],
                             wo=Wo, ident=_IDENT))
    return _gather_halves(_run8(nc2, in_maps2), "xnew")


def run_l2a(x_full, w):
    nc = cached_build("l2a", build_l2a)
    tril = np.tril(np.ones((128, 128), np.float32)).astype(BF)
    in_maps = []
    for c in range(NCORES):
        b = c % 4
        in_maps.append(dict(x=as_np(x_full[b]),
                            wqT=as_np(np.asarray(w["L2_attn_w_q"]).T),
                            wkT=as_np(np.asarray(w["L2_attn_w_k"]).T),
                            wvT=as_np(np.asarray(w["L2_attn_w_v"]).T),
                            woT=as_np(np.asarray(w["L2_attn_w_o"]).T),
                            m_in=as_np(w["L2_attn_M"]),
                            z_in=as_np(w["L2_attn_z"]).reshape(128, 1),
                            gate=as_np(w["L2_attn_gate"]).reshape(1, 1),
                            trild=tril, ident=_IDENT))
    res = _run8(nc, in_maps)
    return np.stack([np.asarray(res.results[b]["xnew"], np.float32)
                     for b in range(4)])


def run_l2f(x_full, w):
    nc = cached_build("l2f", build_l2f)
    in_maps = []
    for c in range(NCORES):
        b, rows, half = _core_slices(c)
        in_maps.append(dict(x=as_np(x_full[b, rows]),
                            gwT=as_np(np.asarray(w["L2_ffn_gate_w"]).T),
                            vwT=as_np(np.asarray(w["L2_ffn_val_w"]).T),
                            owT=as_np(np.asarray(w["L2_ffn_out_w"]).T),
                            gbr=as_np(w["L2_ffn_gate_b"]).reshape(1, -1),
                            vbr=as_np(w["L2_ffn_val_b"]).reshape(1, -1),
                            obr=as_np(w["L2_ffn_out_b"]).reshape(1, -1),
                            identf=_IDENTF))
    return _gather_halves(_run8(nc, in_maps), "xnew")


def run_l3a(x_full, w):
    nc1 = cached_build("l3a1", build_l3a1)
    Wq = as_np(w["L3_attn_Wq"])
    Wk = as_np(w["L3_attn_Wk"])
    Wv = as_np(w["L3_attn_Wv"])
    Wo = as_np(w["L3_attn_Wo"])
    in_maps = []
    for c in range(NCORES):
        b, rows, half = _core_slices(c)
        in_maps.append(dict(x=as_np(x_full[b, rows]),
                            wqT=as_np(Wq.T), wkT=as_np(Wk.T), wvT=as_np(Wv.T),
                            bq=as_np(w["L3_attn_bq"]).reshape(1, D),
                            bk=as_np(w["L3_attn_bk"]).reshape(1, D),
                            bv=as_np(w["L3_attn_bv"]).reshape(1, D)))
    res1 = _run8(nc1, in_maps)
    nc2 = cached_build("l3a2", build_l3a2)
    pos = np.arange(T)
    alibi = -np.abs(pos[None, :] - pos[:, None]).astype(np.float32)
    in_maps2 = []
    for c in range(NCORES):
        b, rows, half = _core_slices(c)
        kfull = np.concatenate([np.asarray(res1.results[2 * b]["k_scr"]),
                                np.asarray(res1.results[2 * b + 1]["k_scr"])], 0)
        vfull = np.concatenate([np.asarray(res1.results[2 * b]["v_scr"]),
                                np.asarray(res1.results[2 * b + 1]["v_scr"])], 0)
        in_maps2.append(dict(x=as_np(x_full[b, rows]),
                             q_scr=np.ascontiguousarray(res1.results[c]["q_scr"]),
                             kf=np.ascontiguousarray(kfull),
                             vf=np.ascontiguousarray(vfull),
                             dmat=as_np(alibi[rows]),
                             slopes=as_np(w["L3_attn_slopes"]).reshape(1, 16),
                             woT=as_np(Wo.T),
                             bo=as_np(w["L3_attn_bo"]).reshape(1, D),
                             ident=_IDENT))
    return _gather_halves(_run8(nc2, in_maps2), "xnew")


def run_l1f(x_full, w):
    nc = cached_build("l1f", build_l1f)
    la = np.asarray(w["L1_ffn_lora_A"], np.float32)
    lb = np.asarray(w["L1_ffn_lora_B"], np.float32)
    in_maps = []
    for c in range(NCORES):
        b, rows, half = _core_slices(c)
        in_maps.append(dict(x=as_np(x_full[b, rows]),
                            wT=as_np(np.asarray(w["L1_ffn_W"]).T),
                            bias=as_np(w["L1_ffn_b"]).reshape(1, D),
                            gwT=as_np(np.asarray(w["L1_ffn_gate_w"]).T),
                            gb=as_np(w["L1_ffn_gate_b"]).reshape(1, 8),
                            aT=as_np(la.transpose(2, 0, 1).reshape(D, 128)),
                            bT=as_np(lb.transpose(0, 2, 1).reshape(128, D)),
                            ident=_IDENT))
    return _gather_halves(_run8(nc, in_maps), "xnew")


def run_l3f(x_full, w):
    nc = cached_build("l3f", build_l3f)
    in_maps = []
    for c in range(NCORES):
        b, rows, half = _core_slices(c)
        in_maps.append(dict(x=as_np(x_full[b, rows]),
                            w1T=as_np(np.asarray(w["L3_ffn_w1"]).T),
                            w2T=as_np(np.asarray(w["L3_ffn_w2"]).T),
                            w3T=as_np(np.asarray(w["L3_ffn_w3"]).T)))
    return _gather_halves(_run8(nc, in_maps), "xnew")


def kernel(**inp):
    x = as_np(inp["L0_attn_x"])
    x = run_l0a(x, inp)
    x = run_l0f(x, inp)
    x = run_l1a(x, inp)
    x = run_l1f(x, inp)
    x = run_l2a(x, inp)
    x = run_l2f(x, inp)
    x = run_l3a(x, inp)
    x = run_l3f(x, inp)
    return x
